# version 05
# baseline (speedup 1.0000x reference)
import numpy as np
from contextlib import ExitStack
import concourse.bass as bass
import concourse.mybir as mybir

F32 = mybir.dt.float32
BF16 = mybir.dt.bfloat16
AF = mybir.ActivationFunctionType
ALU = mybir.AluOpType
AX = mybir.AxisListType

P = 128
T = 2048
D = 1024
NKD = D // P
FC = 512
NFC = T // FC


class C:

    def __init__(self, nc):
        self.nc = nc
        self.es = ExitStack()
        self._stack = [self.es]
        self._sems = {}
        self.cnt = {}
        self.uid = 0

    def u(self, pfx):
        self.uid += 1
        return f"{pfx}{self.uid}"

    def scope(self):
        class _S:
            def __enter__(s):
                es = ExitStack()
                self._stack.append(es)
                return es

            def __exit__(s, *a):
                self._stack.pop().close()
                return False
        return _S()

    def sb(self, name, shape, dt):
        return self._stack[-1].enter_context(
            self.nc.sbuf_tensor(self.u(name), shape, dt))

    def ps(self, name, shape, dt=F32):
        return self._stack[-1].enter_context(
            self.nc.psum_tensor(self.u(name), shape, dt))

    def sem(self, name):
        if name not in self._sems:
            self._sems[name] = self.es.enter_context(self.nc.semaphore(name))
            self.cnt[name] = 0
        return self._sems[name]

    def inc(self, instr, name, k=1):
        s = self.sem(name)
        instr.then_inc(s, k)
        self.cnt[name] += k
        return self.cnt[name]

    def wait(self, eng, name, val=None):
        eng.wait_ge(self.sem(name), self.cnt[name] if val is None else val)

    def fin(self, i, name, eng):
        n = self.inc(i, name)
        eng.wait_ge(self.sem(name), n)
        return i

    def fin_val(self, i, name, eng):
        self.fin(i, name, eng)
        return self.cnt[name]

    def sv(self, i):
        n = self.inc(i, "ssv")
        self.nc.vector.wait_ge(self.sem("ssv"), n)
        return i

    def ssc(self, i):
        n = self.inc(i, "sss")
        self.nc.scalar.wait_ge(self.sem("sss"), n)
        return i

    def sg(self, i):
        n = self.inc(i, "ssg")
        self.nc.gpsimd.wait_ge(self.sem("ssg"), n)
        return i

    def dma_join(self, q=None, names=("t", "v", "s")):
        q = q or self.nc.sync
        for n in names:
            if n in self.cnt and self.cnt[n] > 0:
                self.wait(q, n)

    def close(self):
        self.es.close()


class Consts:

    def __init__(self, c, dram, need=("ident_bf", "ident_f", "tril_f", "sel8",
                                      "sel16", "allon8")):
        nc = c.nc
        shapes = {"ones_col": ([P, 1], F32), "ones_row": ([1, P], F32),
                  "ones_col_bf": ([P, 1], BF16),
                  "ident_bf": ([P, P], BF16), "ident_f": ([P, P], F32),
                  "tril_f": ([P, P], F32), "sel8": ([8, 8 * P], F32),
                  "sel16": ([16, 16 * P], F32), "allon8": ([8, P], F32)}
        need = set(need) | {"ones_col", "ones_row", "ones_col_bf"}
        for nm, (shp, dt) in shapes.items():
            if nm in need:
                t = c.sb(nm, shp, dt)
                setattr(self, nm, t)
                i = nc.sync.dma_start(out=t[:], in_=dram["c_" + nm])
                c.inc(i, "ld", 16)
            else:
                setattr(self, nm, None)
        self.ld_target = c.cnt["ld"]
        self.kone = c.sb("kone", [P, 1], F32)
        i = nc.vector.memset(self.kone[:], 1.0)
        c.fin(i, "v", nc.vector)

    def wait_all(self, c):
        for eng in (c.nc.tensor, c.nc.vector, c.nc.scalar, c.nc.gpsimd):
            c.wait(eng, "ld", self.ld_target)
        c.wait(c.nc.scalar, "v")
        c.wait(c.nc.tensor, "v")


def const_inputs():
    import ml_dtypes
    d = {}
    d["c_ones_col"] = np.ones((P, 1), np.float32)
    d["c_ones_col_bf"] = np.ones((P, 1), ml_dtypes.bfloat16)
    d["c_ones_row"] = np.ones((1, P), np.float32)
    d["c_ident_bf"] = np.eye(P, dtype=ml_dtypes.bfloat16)
    d["c_ident_f"] = np.eye(P, dtype=np.float32)
    d["c_tril_f"] = np.tril(np.ones((P, P), np.float32))
    sel = np.zeros((8, 8 * P), np.float32)
    for e in range(8):
        sel[e, e * P:(e + 1) * P] = 1.0
    d["c_sel8"] = sel
    sel = np.zeros((16, 16 * P), np.float32)
    for s in range(16):
        sel[s, s * P:(s + 1) * P] = 1.0
    d["c_sel16"] = sel
    d["c_allon8"] = np.ones((8, P), np.float32)
    return d


def load_tiles(c, dst_list, src2d, ntile, width, sem="ld", q=None):
    q = q or c.nc.sync
    for k in range(ntile):
        i = q.dma_start(out=dst_list[k][:, :width],
                        in_=src2d[k * P:(k + 1) * P, :width])
        c.inc(i, sem, 16)
    return c.cnt[sem]


def emit_ln(c, cst, x_tiles, out_bf, t_len, gamma=None, beta=None, out_f32=None,
            out_lo=None, eps=1e-5, nkd=NKD):
    nc = c.nc
    dmodel = nkd * P
    with c.scope():
        sq = [c.sb("lnsq", [P, FC], F32) for _ in range(nkd)]
        s1p = c.ps("lns1", [1, FC])
        s2p = c.ps("lns2", [1, FC])
        mrow = c.sb("lnm", [1, FC], F32)
        m2row = c.sb("lnm2", [1, FC], F32)
        vrow = c.sb("lnv", [1, FC], F32)
        rrow = c.sb("lnr", [1, FC], F32)
        mb = c.ps("lnmb", [P, FC])
        rb = c.ps("lnrb", [P, FC])
        tmp = c.sb("lntmp", [P, FC], F32)
        tmp2 = c.sb("lntmp2", [P, FC], F32)
        tmp3 = c.sb("lntmp3", [P, FC], F32)
        nfc = (t_len + FC - 1) // FC
        stats_done = 0
        apply_done = 0
        for f in range(nfc):
            fs = slice(f * FC, min((f + 1) * FC, t_len))
            fl = fs.stop - fs.start
            if f > 0:
                c.wait(nc.scalar, "t", stats_done)
            for k in range(nkd):
                i = nc.scalar.activation(sq[k][:, :fl], x_tiles[k][:, fs], AF.Square)
            c.inc(i, "s")
            if f > 0:
                c.wait(nc.tensor, "v", apply_done)
            c.wait(nc.tensor, "s")
            for k in range(nkd):
                nc.tensor.matmul(s1p[:, :fl], cst.ones_col[:], x_tiles[k][:, fs],
                                 start=(k == 0), stop=(k == nkd - 1))
                i = nc.tensor.matmul(s2p[:, :fl], cst.ones_col[:], sq[k][:, :fl],
                                     start=(k == 0), stop=(k == nkd - 1))
            stats_done = c.inc(i, "t")
            c.wait(nc.vector, "t", stats_done)
            c.sv(nc.vector.tensor_scalar_mul(mrow[:, :fl], s1p[:, :fl], 1.0 / dmodel))
            c.sv(nc.vector.tensor_mul(m2row[:, :fl], mrow[:, :fl], mrow[:, :fl]))
            c.sv(nc.vector.tensor_scalar_add(m2row[:, :fl], m2row[:, :fl], -float(eps)))
            i = nc.vector.scalar_tensor_tensor(vrow[:, :fl], s2p[:, :fl],
                                               1.0 / dmodel, m2row[:, :fl],
                                               ALU.mult, ALU.subtract)
            c.fin(i, "v", nc.vector)
            c.wait(nc.scalar, "v")
            c.ssc(nc.scalar.activation(vrow[:, :fl], vrow[:, :fl], AF.Ln))
            i = nc.scalar.activation(rrow[:, :fl], vrow[:, :fl], AF.Exp, scale=-0.5)
            c.fin(i, "s", nc.scalar)
            c.wait(nc.tensor, "s")
            nc.tensor.matmul(mb[:, :fl], cst.ones_row[:], mrow[:, :fl],
                             start=True, stop=True)
            i = nc.tensor.matmul(rb[:, :fl], cst.ones_row[:], rrow[:, :fl],
                                 start=True, stop=True)
            c.inc(i, "t")
            c.wait(nc.vector, "t")
            for k in range(nkd):
                c.sv(nc.vector.tensor_sub(tmp[:, :fl], x_tiles[k][:, fs], mb[:, :fl]))
                if gamma is not None:
                    c.sv(nc.vector.tensor_mul(tmp[:, :fl], tmp[:, :fl], rb[:, :fl]))
                    if out_bf is not None:
                        i = nc.vector.tensor_scalar(out_bf[k][:, fs], tmp[:, :fl],
                                                    gamma[k], beta[k],
                                                    ALU.mult, ALU.add)
                    if out_f32 is not None:
                        i = nc.vector.tensor_scalar(out_f32[k][:, fs],
                                                    tmp[:, :fl],
                                                    gamma[k], beta[k],
                                                    ALU.mult, ALU.add)
                elif out_bf is None:
                    i = nc.vector.tensor_mul(out_f32[k][:, fs], tmp[:, :fl],
                                             rb[:, :fl])
                else:
                    if out_lo is not None:
                        c.sv(nc.vector.tensor_mul(tmp3[:, :fl], tmp[:, :fl],
                                                  rb[:, :fl]))
                        c.sv(nc.vector.tensor_copy(out_bf[k][:, fs], tmp3[:, :fl]))
                        c.sv(nc.vector.tensor_copy(tmp2[:, :fl], out_bf[k][:, fs]))
                        c.sv(nc.vector.tensor_sub(tmp2[:, :fl], tmp3[:, :fl],
                                                  tmp2[:, :fl]))
                        i = nc.vector.tensor_copy(out_lo[k][:, fs], tmp2[:, :fl])
                    else:
                        i = nc.vector.tensor_mul(out_bf[k][:, fs], tmp[:, :fl],
                                                 rb[:, :fl])
                    if out_f32 is not None:
                        i = nc.vector.tensor_mul(out_f32[k][:, fs],
                                                 tmp[:, :fl], rb[:, :fl])
            apply_done = c.fin_val(i, "v", nc.vector)


def emit_ffn0(c, cst, dram, x_in, x_out, dbg=None):
    nc = c.nc
    M = 1408
    NKM = M // P
    with c.scope():
        yacc = [c.sb("f0y", [P, T], F32) for _ in range(NKD)]
        xn = [c.sb("f0xn", [P, T], BF16) for _ in range(NKD)]
        gatT = c.sb("f0gat", [8, T], F32)
        xlo_scope = c.scope()
        xlo_es = xlo_scope.__enter__()
        xn_lo = [c.sb("f0xlo", [P, T], BF16) for _ in range(NKD)]
        scdump = c.sb("f0scd", [P, 8 * (T // P)], F32) if dbg and "scd" in dbg else None
        c.dma_join()
        if "st" in c.cnt:
            c.wait(nc.sync, "st")
        ld_x = load_tiles(c, yacc, x_in, NKD, T)
        for eng in (nc.tensor, nc.vector, nc.scalar):
            c.wait(eng, "ld", ld_x)
        emit_ln(c, cst, yacc, xn, T, out_lo=xn_lo)
        if dbg and "xlo0" in dbg:
            c.wait(nc.gpsimd, "v")
            i = nc.gpsimd.dma_start(out=dbg["xlo0"], in_=xn_lo[0][:])
            c.inc(i, "st", 16)
            i = nc.gpsimd.dma_start(out=dbg["xhi0"], in_=xn[0][:])
            c.inc(i, "st", 16)
        with c.scope():
            wgf = c.sb("f0wgf", [P, 8 * NKD], F32)
            wgb = c.sb("f0wgb", [P, 8 * NKD], BF16)
            wglo = c.sb("f0wglo", [P, 8 * NKD], BF16)
            c.dma_join()
            for k in range(NKD):
                i = nc.sync.dma_start(out=wgf[:, 8 * k:8 * k + 8],
                                      in_=dram["L0_ffn_w_gate"][k * P:(k + 1) * P, :])
                c.inc(i, "ld", 16)
            wgb_up = c.sb("f0wgbu", [P, 8 * NKD], F32)
            c.wait(nc.vector, "ld")
            c.sv(nc.vector.tensor_copy(wgb[:], wgf[:]))
            c.sv(nc.vector.tensor_copy(wgb_up[:], wgb[:]))
            c.sv(nc.vector.tensor_sub(wgb_up[:], wgf[:], wgb_up[:]))
            i = nc.vector.tensor_copy(wglo[:], wgb_up[:])
            c.fin(i, "v", nc.vector)
            c.wait(nc.tensor, "v")
            scp = c.ps("f0scp", [P, 8])
            gtp = c.ps("f0gtp", [8, P])
            sc_s = c.sb("f0sc", [P, 8], F32)
            d1 = c.sb("f0d1", [P, 8], F32)
            oh1 = c.sb("f0oh1", [P, 8], F32)
            msk = c.sb("f0msk", [P, 8], F32)
            em = c.sb("f0em", [P, 8], F32)
            kp2 = c.sb("f0kp2", [P, 8], F32)
            m1 = c.sb("f0m1", [P, 1], F32)
            nm1 = c.sb("f0nm1", [P, 1], F32)
            m2 = c.sb("f0m2", [P, 1], F32)
            nm2 = c.sb("f0nm2", [P, 1], F32)
            dq = c.sb("f0dq", [P, 1], F32)
            qq = c.sb("f0qq", [P, 1], F32)
            den = c.sb("f0den", [P, 1], F32)
            rec = c.sb("f0rec", [P, 1], F32)
            for tt in range(T // P):
                ts_ = slice(tt * P, (tt + 1) * P)
                for k in range(NKD):
                    nc.tensor.matmul(scp[:], xn[k][:, ts_], wgb[:, 8 * k:8 * k + 8],
                                     start=(k == 0), stop=False)
                    nc.tensor.matmul(scp[:], xn[k][:, ts_], wglo[:, 8 * k:8 * k + 8],
                                     start=False, stop=False)
                    i = nc.tensor.matmul(scp[:], xn_lo[k][:, ts_],
                                         wgb[:, 8 * k:8 * k + 8],
                                         start=False, stop=(k == NKD - 1))
                c.inc(i, "t")
                c.wait(nc.vector, "t")
                v = nc.vector
                c.sv(v.tensor_copy(sc_s[:], scp[:]))
                if scdump is not None:
                    c.sv(v.tensor_copy(scdump[:, 8 * tt:8 * tt + 8], sc_s[:]))
                c.sv(v.reduce_max(m1[:, 0:1], sc_s[:], axis=AX.X))
                c.sv(v.tensor_scalar_mul(nm1[:, 0:1], m1[:, 0:1], -1.0))
                c.sv(v.tensor_scalar_add(d1[:], sc_s[:], nm1[:, 0:1]))
                c.sv(v.tensor_single_scalar(oh1[:], d1[:], 0.0, ALU.is_ge))
                c.sv(v.scalar_tensor_tensor(msk[:], oh1[:], -1e30, sc_s[:],
                                            ALU.mult, ALU.add))
                c.sv(v.reduce_max(m2[:, 0:1], msk[:], axis=AX.X))
                c.sv(v.tensor_scalar_mul(nm2[:, 0:1], m2[:, 0:1], -1.0))
                i = v.tensor_sub(dq[:, 0:1], nm1[:, 0:1], nm2[:, 0:1])
                c.fin(i, "v", nc.vector)
                c.wait(nc.scalar, "v")
                c.ssc(nc.scalar.activation(em[:], d1[:], AF.Exp))
                i = nc.scalar.activation(qq[:, 0:1], dq[:, 0:1], AF.Exp)
                c.fin(i, "s", nc.scalar)
                c.wait(nc.vector, "s")
                c.sv(v.tensor_scalar_add(den[:, 0:1], qq[:, 0:1], 1.0))
                c.sv(v.reciprocal(rec[:, 0:1], den[:, 0:1]))
                c.sv(v.tensor_scalar_add(d1[:], sc_s[:], nm2[:, 0:1]))
                c.sv(v.tensor_single_scalar(kp2[:], d1[:], 0.0, ALU.is_ge))
                c.sv(v.tensor_mul(em[:], em[:], kp2[:]))
                i = v.tensor_scalar_mul(em[:], em[:], rec[:, 0:1])
                c.fin(i, "v", nc.vector)
                c.wait(nc.tensor, "v")
                i = nc.tensor.transpose(gtp[:], em[:], cst.ident_f[:])
                c.inc(i, "t")
                c.wait(nc.vector, "t")
                i = v.tensor_copy(gatT[:, ts_], gtp[:])
                c.inc(i, "v")
                c.wait(nc.tensor, "v")
        if dbg and "gatT" in dbg:
            c.wait(nc.gpsimd, "v")
            i = nc.gpsimd.dma_start(out=dbg["gatT"], in_=gatT[:])
            c.inc(i, "st", 16)
            if "scd" in dbg:
                i = nc.gpsimd.dma_start(out=dbg["scd"], in_=scdump[:])
                c.inc(i, "st", 16)
        xlo_scope.__exit__(None, None, None)
        with c.scope():
            wl = [c.sb("f0wl", [P, M], F32) for _ in range(2)]
            wg_bf = [c.sb("f0wgbf", [P, M], BF16) for _ in range(NKD)]
            wu_bf = [c.sb("f0wubf", [P, M], BF16) for _ in range(NKD)]
            wd_bf = [c.sb("f0wdbf", [P, D], BF16) for _ in range(NKM)]
            hg = c.sb("f0hg", [P, FC], BF16)
            hid2 = [c.sb("f0hid", [P, FC], BF16) for _ in range(NKM)]
            pg = c.ps("f0pg", [P, FC])
            pu = c.ps("f0pu", [P, FC])
            pgb = c.ps("f0pgb", [P, FC])
            py = [c.ps("f0py", [P, FC]) for _ in range(2)]
            py_free = [0, 0]
            c.dma_join()
            cast_done = []
            for e in range(8):

                def stream(dst_list, src3, ntile, width):
                    for k in range(ntile):
                        if len(cast_done) >= 2:
                            c.wait(nc.sync, "v", cast_done[-2])
                        i = nc.sync.dma_start(out=wl[len(cast_done) % 2][:, :width],
                                              in_=src3[k * P:(k + 1) * P, :width])
                        c.inc(i, "ld", 16)
                        c.wait(nc.vector, "ld")
                        i = nc.vector.tensor_copy(dst_list[k][:, :width],
                                                  wl[len(cast_done) % 2][:, :width])
                        cast_done.append(c.inc(i, "v"))
                stream(wg_bf, dram["L0_ffn_w_g"][e], NKD, M)
                stream(wu_bf, dram["L0_ffn_w_u"][e], NKD, M)
                stream(wd_bf, dram["L0_ffn_w_d"][e], NKM, D)
                c.wait(nc.tensor, "v")
                for f in range(NFC):
                    fs = slice(f * FC, (f + 1) * FC)
                    i = nc.tensor.matmul(pgb[:], cst.sel8[:, e * P:(e + 1) * P],
                                         gatT[:, fs], start=True, stop=True)
                    for m in range(NKM):
                        ms = slice(m * P, (m + 1) * P)
                        for k in range(NKD):
                            nc.tensor.matmul(pg[:], wg_bf[k][:, ms], xn[k][:, fs],
                                             start=(k == 0), stop=(k == NKD - 1))
                        for k in range(NKD):
                            i = nc.tensor.matmul(pu[:], wu_bf[k][:, ms], xn[k][:, fs],
                                                 start=(k == 0), stop=(k == NKD - 1))
                        c.inc(i, "t")
                        c.wait(nc.scalar, "t")
                        i = nc.scalar.activation(hg[:], pg[:], AF.Silu)
                        c.inc(i, "s")
                        c.wait(nc.vector, "s")
                        c.sv(nc.vector.tensor_tensor(hid2[m][:], hg[:], pgb[:], ALU.mult))
                        i = nc.vector.tensor_tensor(hid2[m][:], hid2[m][:], pu[:],
                                                    ALU.mult)
                        c.fin(i, "v", nc.vector)
                        c.wait(nc.tensor, "v")
                    if dbg and "hid0" in dbg and e == 0 and f == 0:
                        c.wait(nc.gpsimd, "v")
                        for m in range(NKM):
                            i = nc.gpsimd.dma_start(
                                out=dbg["hid0"][m * P:(m + 1) * P, :], in_=hid2[m][:])
                            c.inc(i, "st", 16)
                        c.wait(nc.tensor, "st")
                    for mo in range(NKD):
                        mos = slice(mo * P, (mo + 1) * P)
                        pyt = py[mo % 2]
                        c.wait(nc.tensor, "v", py_free[mo % 2])
                        for k in range(NKM):
                            i = nc.tensor.matmul(pyt[:], wd_bf[k][:, mos], hid2[k][:],
                                                 start=(k == 0), stop=(k == NKM - 1))
                        c.inc(i, "t")
                        c.wait(nc.vector, "t")
                        i = nc.vector.scalar_tensor_tensor(
                            yacc[mo][:, fs], pyt[:], 0.5, yacc[mo][:, fs],
                            ALU.mult, ALU.add)
                        py_free[mo % 2] = c.inc(i, "v")
                    c.wait(nc.tensor, "v")
                if dbg and "y0" in dbg and e == 0:
                    c.wait(nc.gpsimd, "v")
                    for k in range(NKD):
                        i = nc.gpsimd.dma_start(out=dbg["y0"][k * P:(k + 1) * P, :],
                                                in_=yacc[k][:])
                        c.inc(i, "st", 16)
                    c.wait(nc.tensor, "st")
        c.wait(nc.gpsimd, "v")
        for k in range(NKD):
            i = nc.gpsimd.dma_start(out=x_out[k * P:(k + 1) * P, :], in_=yacc[k][:])
            c.inc(i, "st", 16)
    return c.cnt["st"]


def finish(c):
    c.wait(c.nc.gpsimd, "st")


def build_program(phase_fns, input_specs, output_specs, consts_need=None):
    nc = bass.Bass("TRN2", target_bir_lowering=False)
    dram = {}
    for name, (shape, dt) in input_specs.items():
        dram[name] = nc.dram_tensor(name, list(shape), dt, kind="ExternalInput").ap()
    for name, (shape, dt) in output_specs.items():
        dram[name] = nc.dram_tensor(name, list(shape), dt, kind="ExternalOutput").ap()
    c = C(nc)
    block = c.es.enter_context(nc.Block())

    @block.sync
    def _(sync):
        cst = (Consts(c, dram) if consts_need is None
               else Consts(c, dram, need=consts_need))
        cst.wait_all(c)
        for fn in phase_fns:
            fn(c, cst, dram)
        finish(c)
    c.close()
    return nc


def emit_mamba(c, cst, dram, x_in, x_out, dbg=None):
    nc = c.nc
    DI = 2048
    NDT = DI // P
    TC = 256
    NCH = T // TC
    with c.scope():
        lng = c.sb("mb_lng", [P, NKD], F32)
        lnb = c.sb("mb_lnb", [P, NKD], F32)
        cw = c.sb("mb_cw", [P, 4 * NDT], F32)
        cb = c.sb("mb_cb", [P, NDT], F32)
        bd = c.sb("mb_bd", [P, NDT], F32)
        dp = c.sb("mb_dp", [P, NDT], F32)
        logA = c.sb("mb_logA", [P, 16 * NDT], F32)
        A_sb = c.sb("mb_A", [P, 16 * NDT], F32)
        wb_bf = [c.sb("mb_wb", [P, 16], BF16) for _ in range(NDT)]
        wc_bf = [c.sb("mb_wc", [P, 16], BF16) for _ in range(NDT)]
        wsm = c.sb("mb_wsm", [P, 16 * 2], F32)
        carry_h = [c.sb("mb_ch", [P, 16], F32) for _ in range(NDT)]
        carry3 = [c.sb("mb_c3", [P, 4], F32) for _ in range(NDT)]
        c.dma_join()
        for k in range(NKD):
            i = nc.sync.dma_start(out=lng[:, k:k + 1], in_=dram["L0_attn_ln_g"][k])
            c.inc(i, "ld", 16)
            i = nc.sync.dma_start(out=lnb[:, k:k + 1], in_=dram["L0_attn_ln_b"][k])
            c.inc(i, "ld", 16)
        for dt in range(NDT):
            i = nc.sync.dma_start(out=cw[:, 4 * dt:4 * dt + 4],
                                  in_=dram["L0_attn_conv_w"][dt])
            c.inc(i, "ld", 16)
            i = nc.sync.dma_start(out=cb[:, dt:dt + 1], in_=dram["L0_attn_conv_b"][dt])
            c.inc(i, "ld", 16)
            i = nc.sync.dma_start(out=bd[:, dt:dt + 1], in_=dram["L0_attn_b_delta"][dt])
            c.inc(i, "ld", 16)
            i = nc.sync.dma_start(out=dp[:, dt:dt + 1], in_=dram["L0_attn_Dp"][dt])
            c.inc(i, "ld", 16)
            i = nc.sync.dma_start(out=logA[:, 16 * dt:16 * dt + 16],
                                  in_=dram["L0_attn_log_A"][dt])
            c.inc(i, "ld", 16)
            i = nc.sync.dma_start(out=wsm[:, 0:16],
                                  in_=dram["L0_attn_W_b"][dt * P:(dt + 1) * P, :])
            c.inc(i, "ld", 16)
            c.wait(nc.vector, "ld")
            c.sv(nc.vector.tensor_copy(wb_bf[dt][:], wsm[:, 0:16]))
            i = nc.sync.dma_start(out=wsm[:, 16:32],
                                  in_=dram["L0_attn_W_c"][dt * P:(dt + 1) * P, :])
            c.inc(i, "ld", 16)
            c.wait(nc.vector, "ld")
            i = nc.vector.tensor_copy(wc_bf[dt][:], wsm[:, 16:32])
            c.fin(i, "v", nc.vector)
            c.dma_join(names=("v",))
        c.wait(nc.scalar, "ld")
        i = nc.scalar.activation(A_sb[:], logA[:], AF.Exp)
        c.fin(i, "s", nc.scalar)
        c.wait(nc.vector, "s")
        c.sv(nc.vector.tensor_scalar_mul(A_sb[:], A_sb[:], -1.0))
        for dt in range(NDT):
            c.sv(nc.vector.memset(carry_h[dt][:], 0.0))
            i = nc.vector.memset(carry3[dt][:], 0.0)
        c.fin(i, "v", nc.vector)
        x_sb = [c.sb("mb_x", [P, TC], F32) for _ in range(NKD)]
        u_sb = [c.sb("mb_u0", [P, TC], F32) for _ in range(NKD)]
        xnb = [c.sb("mb_xn", [P, TC], BF16) for _ in range(NKD)]
        xb_f = [c.sb("mb_xbf", [P, TC], F32) for _ in range(NDT)]
        xb_b = [c.sb("mb_xbb", [P, TC], BF16) for _ in range(NDT)]
        sz_b = [c.sb("mb_sz", [P, TC], BF16) for _ in range(NDT)]
        delta = [c.sb("mb_dl", [P, TC], F32) for _ in range(NDT)]
        yT = [c.sb("mb_yT", [P, TC], F32) for _ in range(NDT)]
        ygb = xb_b
        BT = c.sb("mb_BT", [16, TC], F32)
        CT = c.sb("mb_CT", [16, TC], F32)
        xbr = c.sb("mb_xbr", [P, TC + 4], F32)
        xc_t = c.sb("mb_xc", [P, TC], F32)
        a_buf = c.sb("mb_a", [P, TC], F32)
        u_buf = c.sb("mb_u", [P, TC], F32)
        h_buf = c.sb("mb_h", [P, TC], F32)
        sp1 = a_buf
        sp2 = u_buf
        for ch in range(NCH):
            cs = slice(ch * TC, (ch + 1) * TC)
            c.dma_join()
            if "st" in c.cnt:
                c.wait(nc.sync, "st")
            for k in range(NKD):
                i = nc.sync.dma_start(out=x_sb[k][:],
                                      in_=x_in[k * P:(k + 1) * P, cs])
                c.inc(i, "ld", 16)
            ldx = c.cnt["ld"]
            for eng in (nc.tensor, nc.vector, nc.scalar):
                c.wait(eng, "ld", ldx)
            emit_ln(c, cst, x_sb, None, TC, out_f32=u_sb)
            c.wait(nc.scalar, "v")
            c.wait(nc.tensor, "v")
            emit_ln(c, cst, u_sb, xnb, TC,
                    gamma=[lng[:, k:k + 1] for k in range(NKD)],
                    beta=[lnb[:, k:k + 1] for k in range(NKD)])
            for half in range(2):
                with c.scope():
                    wl = [c.sb("mb_wl", [P, 2048], F32) for _ in range(2)]
                    win_bf = [c.sb("mb_win", [P, 2048], BF16) for _ in range(NKD)]
                    cd = []
                    c.dma_join()
                    for k in range(NKD):
                        if len(cd) >= 2:
                            c.wait(nc.sync, "v", cd[-2])
                        i = nc.sync.dma_start(
                            out=wl[k % 2][:],
                            in_=dram["L0_attn_W_in"][k * P:(k + 1) * P,
                                                     half * 2048:(half + 1) * 2048])
                        c.inc(i, "ld", 16)
                        c.wait(nc.vector, "ld")
                        i = nc.vector.tensor_copy(win_bf[k][:], wl[k % 2][:])
                        cd.append(c.inc(i, "v"))
                    c.wait(nc.tensor, "v")
                    pz = c.ps("mb_pz", [P, TC])
                    for m in range(16):
                        dt = m
                        ms = slice(m * P, (m + 1) * P)
                        for k in range(NKD):
                            i = nc.tensor.matmul(pz[:], win_bf[k][:, ms],
                                                 xnb[k][:], start=(k == 0),
                                                 stop=(k == NKD - 1))
                        c.inc(i, "t")
                        if half == 1:
                            c.wait(nc.scalar, "t")
                            i = nc.scalar.activation(sz_b[dt][:], pz[:], AF.Silu)
                            c.fin(i, "s", nc.scalar)
                            c.wait(nc.tensor, "s")
                        else:
                            c.wait(nc.vector, "t")
                            c.sv(nc.vector.tensor_copy(xbr[:, 0:3],
                                                       carry3[dt][:, 0:3]))
                            c.sv(nc.vector.tensor_copy(xbr[:, 3:TC + 3], pz[:]))
                            c.sv(nc.vector.tensor_copy(carry3[dt][:, 0:3],
                                                       xbr[:, TC:TC + 3]))
                            c.sv(nc.vector.tensor_scalar_mul(
                                xc_t[:], xbr[:, 0:TC], cw[:, 4 * dt:4 * dt + 1]))
                            for j in range(1, 4):
                                c.sv(nc.vector.scalar_tensor_tensor(
                                    xc_t[:], xbr[:, j:j + TC],
                                    cw[:, 4 * dt + j:4 * dt + j + 1], xc_t[:],
                                    ALU.mult, ALU.add))
                            i = nc.vector.engine_nop()
                            c.fin(i, "v", nc.vector)
                            c.wait(nc.scalar, "v")
                            i = nc.scalar.activation(xb_f[dt][:], xc_t[:], AF.Silu,
                                                     bias=cb[:, dt:dt + 1])
                            c.fin(i, "s", nc.scalar)
                            c.wait(nc.vector, "s")
                            i = nc.vector.tensor_copy(xb_b[dt][:], xb_f[dt][:])
                            c.fin(i, "v", nc.vector)
                            c.wait(nc.tensor, "v")
            if dbg and ch == 0:
                c.wait(nc.gpsimd, "v")
                c.wait(nc.gpsimd, "s")
                for k in range(NKD):
                    i = nc.gpsimd.dma_start(out=dbg["u0"][k * P:(k + 1) * P, :],
                                            in_=u_sb[k][:])
                    c.inc(i, "st", 16)
                for dt in range(NDT):
                    i = nc.gpsimd.dma_start(out=dbg["xb0"][dt * P:(dt + 1) * P, :],
                                            in_=xb_f[dt][:])
                    c.inc(i, "st", 16)
                    i = nc.gpsimd.dma_start(out=dbg["sz0"][dt * P:(dt + 1) * P, :],
                                            in_=sz_b[dt][:])
                    c.inc(i, "st", 16)
                c.wait(nc.tensor, "st")
            with c.scope():
                pbc = c.ps("mb_pbc", [16, TC])
                for dt in range(NDT):
                    i = nc.tensor.matmul(pbc[:], wb_bf[dt][:], xb_b[dt][:],
                                         start=(dt == 0), stop=(dt == NDT - 1))
                c.inc(i, "t")
                c.wait(nc.vector, "t")
                i = nc.vector.tensor_copy(BT[:], pbc[:])
                c.fin(i, "v", nc.vector)
                c.wait(nc.tensor, "v")
                for dt in range(NDT):
                    i = nc.tensor.matmul(pbc[:], wc_bf[dt][:], xb_b[dt][:],
                                         start=(dt == 0), stop=(dt == NDT - 1))
                c.inc(i, "t")
                c.wait(nc.vector, "t")
                i = nc.vector.tensor_copy(CT[:], pbc[:])
                c.fin(i, "v", nc.vector)
            for half in range(2):
                with c.scope():
                    wl = [c.sb("mb_wdl", [P, 1024], F32) for _ in range(2)]
                    wd_bf = [c.sb("mb_wd", [P, 1024], BF16) for _ in range(NDT)]
                    cd = []
                    c.dma_join()
                    for k in range(NDT):
                        if len(cd) >= 2:
                            c.wait(nc.sync, "v", cd[-2])
                        i = nc.sync.dma_start(
                            out=wl[k % 2][:],
                            in_=dram["L0_attn_W_delta"][k * P:(k + 1) * P,
                                                        half * 1024:(half + 1) * 1024])
                        c.inc(i, "ld", 16)
                        c.wait(nc.vector, "ld")
                        i = nc.vector.tensor_copy(wd_bf[k][:], wl[k % 2][:])
                        cd.append(c.inc(i, "v"))
                    c.wait(nc.tensor, "v")
                    pd = c.ps("mb_pd", [P, TC])
                    for m in range(8):
                        dt = half * 8 + m
                        ms = slice(m * P, (m + 1) * P)
                        for k in range(NDT):
                            i = nc.tensor.matmul(pd[:], wd_bf[k][:, ms],
                                                 xb_b[k][:], start=(k == 0),
                                                 stop=(k == NDT - 1))
                        c.inc(i, "t")
                        c.wait(nc.scalar, "t")
                        c.ssc(nc.scalar.activation(sp1[:], pd[:], AF.Relu,
                                                   bias=bd[:, dt:dt + 1]))
                        c.ssc(nc.scalar.activation(sp2[:], pd[:], AF.Abs,
                                                   bias=bd[:, dt:dt + 1]))
                        c.ssc(nc.scalar.activation(sp2[:], sp2[:], AF.Exp,
                                                   scale=-1.0))
                        i = nc.scalar.activation(sp2[:], sp2[:], AF.Ln,
                                                 bias=cst.kone[:, 0:1])
                        c.fin(i, "s", nc.scalar)
                        c.wait(nc.vector, "s")
                        i = nc.vector.tensor_add(delta[dt][:], sp1[:], sp2[:])
                        c.fin(i, "v", nc.vector)
                        c.wait(nc.tensor, "v")
                        c.wait(nc.scalar, "v")
            if dbg and ch == 0:
                c.wait(nc.gpsimd, "v")
                i = nc.gpsimd.dma_start(out=dbg["BT0"], in_=BT[:])
                c.inc(i, "st", 16)
                i = nc.gpsimd.dma_start(out=dbg["CT0"], in_=CT[:])
                c.inc(i, "st", 16)
                for dt in range(NDT):
                    i = nc.gpsimd.dma_start(out=dbg["dl0"][dt * P:(dt + 1) * P, :],
                                            in_=delta[dt][:])
                    c.inc(i, "st", 16)
                c.wait(nc.tensor, "st")
            with c.scope():
                pB = c.ps("mb_pB", [P, TC])
                pC = c.ps("mb_pC", [P, TC])
                for s in range(16):
                    nc.tensor.matmul(pB[:], cst.sel16[:, s * P:(s + 1) * P], BT[:],
                                     start=True, stop=True)
                    i = nc.tensor.matmul(pC[:], cst.sel16[:, s * P:(s + 1) * P],
                                         CT[:], start=True, stop=True)
                    c.inc(i, "t")
                    c.wait(nc.vector, "t")
                    c.wait(nc.scalar, "t")
                    for dt in range(NDT):
                        c.wait(nc.scalar, "v")
                        i = nc.scalar.activation(a_buf[:], delta[dt][:], AF.Exp,
                                                 scale=A_sb[:, 16 * dt + s:
                                                            16 * dt + s + 1])
                        c.fin(i, "s", nc.scalar)
                        c.wait(nc.vector, "s")
                        c.sv(nc.vector.tensor_tensor(u_buf[:], delta[dt][:],
                                                     pB[:], ALU.mult))
                        c.sv(nc.vector.tensor_tensor(u_buf[:], u_buf[:],
                                                     xb_f[dt][:], ALU.mult))
                        c.sv(nc.vector.tensor_tensor_scan(
                            h_buf[:], a_buf[:], u_buf[:],
                            carry_h[dt][:, s:s + 1], ALU.mult, ALU.add))
                        c.sv(nc.vector.tensor_copy(carry_h[dt][:, s:s + 1],
                                                   h_buf[:, TC - 1:TC]))
                        if s == 0:
                            i = nc.vector.tensor_tensor(yT[dt][:], h_buf[:],
                                                        pC[:], ALU.mult)
                        else:
                            c.sv(nc.vector.tensor_tensor(u_buf[:], h_buf[:],
                                                         pC[:], ALU.mult))
                            i = nc.vector.tensor_add(yT[dt][:], yT[dt][:],
                                                     u_buf[:])
                        c.fin(i, "v", nc.vector)
                    c.wait(nc.tensor, "v")
            if dbg and ch == 0:
                c.wait(nc.gpsimd, "v")
                for dt in range(NDT):
                    i = nc.gpsimd.dma_start(out=dbg["yT0"][dt * P:(dt + 1) * P, :],
                                            in_=yT[dt][:])
                    c.inc(i, "st", 16)
                c.wait(nc.tensor, "st")
                c.wait(nc.vector, "st")
            for dt in range(NDT):
                c.sv(nc.vector.scalar_tensor_tensor(
                    yT[dt][:], xb_f[dt][:], dp[:, dt:dt + 1], yT[dt][:],
                    ALU.mult, ALU.add))
                c.sv(nc.vector.tensor_copy(a_buf[:], sz_b[dt][:]))
                i = nc.vector.tensor_tensor(ygb[dt][:], yT[dt][:], a_buf[:],
                                            ALU.mult)
                c.fin(i, "v", nc.vector)
            if dbg and ch == 0:
                c.wait(nc.gpsimd, "v")
                for dt in range(NDT):
                    i = nc.gpsimd.dma_start(out=dbg["yg0"][dt * P:(dt + 1) * P, :],
                                            in_=ygb[dt][:])
                    c.inc(i, "st", 16)
                c.wait(nc.tensor, "st")
                c.wait(nc.vector, "st")
            with c.scope():
                wl = [c.sb("mb_wol", [P, 1024], F32) for _ in range(2)]
                wo_bf = [c.sb("mb_wo", [P, 1024], BF16) for _ in range(NDT)]
                cd = []
                c.dma_join()
                for k in range(NDT):
                    if len(cd) >= 2:
                        c.wait(nc.sync, "v", cd[-2])
                    i = nc.sync.dma_start(
                        out=wl[k % 2][:],
                        in_=dram["L0_attn_W_out"][k * P:(k + 1) * P, :])
                    c.inc(i, "ld", 16)
                    c.wait(nc.vector, "ld")
                    i = nc.vector.tensor_copy(wo_bf[k][:], wl[k % 2][:])
                    cd.append(c.inc(i, "v"))
                c.wait(nc.tensor, "v")
                po = c.ps("mb_po", [P, TC])
                for mo in range(NKD):
                    mos = slice(mo * P, (mo + 1) * P)
                    for k in range(NDT):
                        i = nc.tensor.matmul(po[:], wo_bf[k][:, mos], ygb[k][:],
                                             start=(k == 0), stop=(k == NDT - 1))
                    c.inc(i, "t")
                    c.wait(nc.vector, "t")
                    c.sv(nc.vector.tensor_add(x_sb[mo][:], x_sb[mo][:], po[:]))
                    i = nc.vector.tensor_add(x_sb[mo][:], x_sb[mo][:],
                                             u_sb[mo][:])
                    c.fin(i, "v", nc.vector)
                    c.wait(nc.tensor, "v")
                c.wait(nc.gpsimd, "v")
                for k in range(NKD):
                    i = nc.gpsimd.dma_start(out=x_out[k * P:(k + 1) * P, cs],
                                            in_=x_sb[k][:])
                    c.inc(i, "st", 16)


def emit_l1router(c, cst, dram, x_in, hid_out, emb_out, oh_out, rp_out, cnt_out):
    nc = c.nc
    H, DH = 16, 64
    with c.scope():
        embed_bf = [c.sb("r_emb", [P, T], BF16) for _ in range(NKD)]
        with c.scope():
            x_sb = [c.sb("r_x", [P, T], F32) for _ in range(NKD)]
            embf = [c.sb("r_ef", [P, T], F32) for _ in range(NKD)]
            c.dma_join()
            if "st" in c.cnt:
                c.wait(nc.sync, "st")
            ldx = load_tiles(c, x_sb, x_in, NKD, T)
            for eng in (nc.tensor, nc.vector, nc.scalar):
                c.wait(eng, "ld", ldx)
            emit_ln(c, cst, x_sb, embed_bf, T, out_f32=embf)
            c.wait(nc.gpsimd, "v")
            for k in range(NKD):
                i = nc.gpsimd.dma_start(out=emb_out[k * P:(k + 1) * P, :],
                                        in_=embf[k][:])
                c.inc(i, "st", 16)
        hidT = [c.sb("r_hid", [P, T], BF16) for _ in range(NKD)]
        bq = c.sb("r_bq", [DH, H], F32)
        bq32 = c.sb("r_bq32", [DH, H], F32)
        bk = c.sb("r_bk", [DH, H], F32)
        bvr = c.sb("r_bvr", [1, DH * H], F32)
        c.dma_join(names=("v",))
        c.wait(nc.sync, "st")
        for h in range(H):
            i = nc.sync.dma_start(out=bq[:, h:h + 1], in_=dram["L1_ffn_bq"][h])
            c.inc(i, "ld", 16)
            i = nc.sync.dma_start(out=bk[:, h:h + 1], in_=dram["L1_ffn_bk"][h])
            c.inc(i, "ld", 16)
            i = nc.sync.dma_start(out=bvr[:, h * DH:(h + 1) * DH],
                                  in_=dram["L1_ffn_bv"][h])
            c.inc(i, "ld", 16)
        c.wait(nc.vector, "ld")
        i = nc.vector.tensor_scalar_mul(bq32[:], bq[:], 1.0 / 32.0)
        c.fin(i, "v", nc.vector)
        with c.scope():
            qT = c.sb("r_qT", [DH, T], BF16)
            kT = c.sb("r_kT", [DH, T], BF16)
            V = [c.sb("r_V", [P, DH], BF16) for _ in range(T // P)]
            p_sb = c.sb("r_p", [P, T], BF16)
            pT_sb = c.sb("r_pT", [P, P], BF16)
            bvb = c.sb("r_bvb", [P, DH], F32)
            vcs = c.sb("r_vcs", [1, DH * 18], BF16)
            vcf = c.sb("r_vcf", [1, DH * 17], F32)
            w0 = c.sb("r_w0", [P, 1], F32)
            w0r = c.sb("r_w0r", [1, P], BF16)
            m0a = c.sb("r_m0a", [P, 1], F32)
            m0b = c.sb("r_m0b", [P, 1], F32)
            negm = c.sb("r_negm", [P, 1], F32)
            enm = c.sb("r_enm", [P, 1], F32)
            za = c.sb("r_za", [P, 1], F32)
            zb = c.sb("r_zb", [P, 1], F32)
            rcp = c.sb("r_rcp", [P, 1], F32)
            tmpd = c.sb("r_tmpd", [P, P], F32)
            wl = [c.sb("r_wl", [P, DH], F32) for _ in range(2)]
            wq_bf = [c.sb("r_wqb", [P, DH], BF16) for _ in range(NKD)]
            wk_bf = [c.sb("r_wkb", [P, DH], BF16) for _ in range(NKD)]
            wv_bf = [c.sb("r_wvb", [P, DH], BF16) for _ in range(NKD)]
            cd = []

            def stream(dst_list, src3):
                for k in range(NKD):
                    if len(cd) >= 2:
                        c.wait(nc.sync, "v", cd[-2])
                    i = nc.sync.dma_start(out=wl[len(cd) % 2][:],
                                          in_=src3[k * P:(k + 1) * P, :])
                    c.inc(i, "ld", 16)
                    c.wait(nc.vector, "ld")
                    i = nc.vector.tensor_copy(dst_list[k][:],
                                              wl[len(cd) % 2][:])
                    cd.append(c.inc(i, "v"))
            c.dma_join()
            for h in range(H):
              with c.scope():
                pq = c.ps("r_pq", [DH, FC])
                pv = c.ps("r_pv", [P, DH])
                pvc = c.ps("r_pvc", [1, DH])
                stream(wq_bf, dram["L1_ffn_Wq"][h])
                stream(wk_bf, dram["L1_ffn_Wk"][h])
                stream(wv_bf, dram["L1_ffn_Wv"][h])
                c.wait(nc.tensor, "v")
                for f in range(NFC):
                    fs = slice(f * FC, (f + 1) * FC)
                    for k in range(NKD):
                        i = nc.tensor.matmul(pq[:], wq_bf[k][:], embed_bf[k][:, fs],
                                             start=(k == 0), stop=(k == NKD - 1))
                    c.inc(i, "t")
                    c.wait(nc.scalar, "t")
                    i = nc.scalar.activation(qT[:, fs], pq[:], AF.Identity,
                                             bias=bq32[:, h:h + 1], scale=1.0 / 32)
                    c.fin(i, "s", nc.scalar)
                    c.wait(nc.tensor, "s")
                    for k in range(NKD):
                        i = nc.tensor.matmul(pq[:], wk_bf[k][:], embed_bf[k][:, fs],
                                             start=(k == 0), stop=(k == NKD - 1))
                    c.inc(i, "t")
                    c.wait(nc.scalar, "t")
                    i = nc.scalar.activation(kT[:, fs], pq[:], AF.Identity,
                                             bias=bk[:, h:h + 1])
                    c.fin(i, "s", nc.scalar)
                    c.wait(nc.tensor, "s")
                i = nc.tensor.matmul(pv[:], cst.ones_row[:],
                                     bvr[0:1, h * DH:(h + 1) * DH],
                                     start=True, stop=True)
                c.inc(i, "t")
                c.wait(nc.vector, "t")
                i = nc.vector.tensor_copy(bvb[:], pv[:])
                c.fin(i, "v", nc.vector)
                c.wait(nc.tensor, "v")
                for tc in range(T // P):
                    tcs = slice(tc * P, (tc + 1) * P)
                    for k in range(NKD):
                        i = nc.tensor.matmul(pv[:], embed_bf[k][:, tcs], wv_bf[k][:],
                                             start=(k == 0), stop=(k == NKD - 1))
                    c.inc(i, "t")
                    c.wait(nc.vector, "t")
                    i = nc.vector.tensor_tensor(V[tc][:], pv[:], bvb[:], ALU.add)
                    c.fin(i, "v", nc.vector)
                    c.wait(nc.tensor, "v")
                    i = nc.tensor.matmul(pvc[:], cst.ones_col_bf[:], V[tc][:],
                                         start=True, stop=True)
                    c.inc(i, "t")
                    c.wait(nc.vector, "t")
                    i = nc.vector.tensor_copy(vcf[0:1, tc * DH:(tc + 1) * DH],
                                              pvc[:])
                    c.fin(i, "v", nc.vector)
                    c.wait(nc.tensor, "v")
                c.sv(nc.vector.memset(vcs[0:1, 16 * DH:17 * DH], 0.0))
                c.sv(nc.vector.memset(vcf[0:1, 16 * DH:17 * DH], 0.0))
                for tc in range(15, -1, -1):
                    c.sv(nc.vector.tensor_add(
                        vcf[0:1, tc * DH:(tc + 1) * DH],
                        vcf[0:1, tc * DH:(tc + 1) * DH],
                        vcf[0:1, (tc + 1) * DH:(tc + 2) * DH]))
                    c.sv(nc.vector.tensor_copy(vcs[0:1, tc * DH:(tc + 1) * DH],
                                               vcf[0:1, tc * DH:(tc + 1) * DH]))
                i = nc.vector.engine_nop()
                c.fin(i, "v", nc.vector)
              with c.scope():
                    natt = c.ps("r_att", [P, T])
                    ptp = c.ps("r_ptp", [P, P], BF16)
                    pot = c.ps("r_pot", [DH, P])
                    pw0 = c.ps("r_pw0", [1, P])
                    for qc in range(T // P):
                        L = P * (qc + 1)
                        qs = slice(qc * P, (qc + 1) * P)
                        nblk = (L + FC - 1) // FC
                        for b_ in range(nblk):
                            F = min(FC, L - b_ * FC)
                            i = nc.tensor.matmul(
                                natt[:, b_ * FC:b_ * FC + F], qT[:, qs],
                                kT[:, b_ * FC:b_ * FC + F],
                                start=True, stop=True)
                        c.inc(i, "t")
                        c.wait(nc.vector, "t")
                        c.sv(nc.vector.tensor_tensor(tmpd[:], natt[:, qs],
                                                     cst.tril_f[:], ALU.mult))
                        if qc > 0:
                            c.sv(nc.vector.reduce_max(m0a[:, 0:1],
                                                      natt[:, 0:qc * P], axis=AX.X))
                        c.sv(nc.vector.reduce_max(m0b[:, 0:1], tmpd[:], axis=AX.X))
                        if qc > 0:
                            c.sv(nc.vector.tensor_max(m0b[:, 0:1], m0b[:, 0:1],
                                                      m0a[:, 0:1]))
                        c.sv(nc.vector.tensor_scalar_max(m0b[:, 0:1], m0b[:, 0:1],
                                                         0.0))
                        i = nc.vector.tensor_scalar_mul(negm[:, 0:1], m0b[:, 0:1],
                                                        -1.0)
                        c.fin(i, "v", nc.vector)
                        c.wait(nc.scalar, "v")
                        if qc > 0:
                            c.ssc(nc.scalar.activation(p_sb[:, 0:qc * P],
                                                       natt[:, 0:qc * P], AF.Exp,
                                                       bias=negm[:, 0:1],
                                                       accum_out=za[:, 0:1]))
                        c.ssc(nc.scalar.activation(p_sb[:, qs], tmpd[:], AF.Exp,
                                                   bias=negm[:, 0:1],
                                                   accum_out=zb[:, 0:1]))
                        i = nc.scalar.activation(enm[:, 0:1], negm[:, 0:1], AF.Exp)
                        c.fin(i, "s", nc.scalar)
                        c.wait(nc.vector, "s")
                        if qc > 0:
                            c.sv(nc.vector.tensor_add(zb[:, 0:1], zb[:, 0:1],
                                                      za[:, 0:1]))
                        K0 = float(T - L)
                        if K0 > 0:
                            c.sv(nc.vector.scalar_tensor_tensor(
                                zb[:, 0:1], enm[:, 0:1], K0, zb[:, 0:1],
                                ALU.mult, ALU.add))
                        c.sv(nc.vector.reciprocal(rcp[:, 0:1], zb[:, 0:1]))
                        c.sv(nc.vector.tensor_scalar_mul(p_sb[:, 0:L],
                                                         p_sb[:, 0:L],
                                                         rcp[:, 0:1]))
                        i = nc.vector.tensor_mul(w0[:, 0:1], enm[:, 0:1],
                                                 rcp[:, 0:1])
                        c.fin(i, "v", nc.vector)
                        c.wait(nc.tensor, "v")
                        for tcb in range(qc + 1):
                            i = nc.tensor.transpose(ptp[:],
                                                    p_sb[:, tcb * P:(tcb + 1) * P],
                                                    cst.ident_bf[:])
                            c.inc(i, "t")
                            c.wait(nc.vector, "t")
                            i = nc.vector.tensor_copy(pT_sb[:], ptp[:])
                            c.fin(i, "v", nc.vector)
                            c.wait(nc.tensor, "v")
                            i = nc.tensor.matmul(pot[:], V[tcb][:], pT_sb[:],
                                                 start=(tcb == 0), stop=False)
                        if qc < 15:
                            i = nc.tensor.transpose(pw0[:], w0[:, 0:1],
                                                    cst.ident_f[:])
                            c.inc(i, "t")
                            c.wait(nc.vector, "t")
                            i = nc.vector.tensor_copy(w0r[:], pw0[:])
                            c.fin(i, "v", nc.vector)
                            c.wait(nc.tensor, "v")
                            i = nc.tensor.matmul(
                                pot[:], vcs[0:1, (qc + 1) * DH:(qc + 2) * DH],
                                w0r[:], start=False, stop=True)
                        else:
                            i = nc.tensor.matmul(pot[:], V[qc][:], pT_sb[:],
                                                 start=False, stop=True,
                                                 skip_group_check=True) if False \
                                else i
                        c.inc(i, "t")
                        c.wait(nc.vector, "t")
                        dst = hidT[h // 2][(h % 2) * DH:(h % 2) * DH + DH, qs]
                        i = nc.vector.tensor_copy(dst, pot[:])
                        c.fin(i, "v", nc.vector)
                        c.wait(nc.tensor, "v")
        with c.scope():
            ohT = c.sb("r_ohT", [8, T], F32)
            rpT = c.sb("r_rpT", [1, T], F32)
            wswl = c.sb("r_wswl", [P, 8 * NKD], F32)
            wsw_bf = c.sb("r_wswb", [P, 8 * NKD], BF16)
            bswr = c.sb("r_bswr", [1, 8], F32)
            bswb = c.sb("r_bswb", [P, 8], F32)
            lg = c.sb("r_lg", [P, 8], F32)
            em = c.sb("r_em", [P, 8], F32)
            mx = c.sb("r_mx", [P, 1], F32)
            ngm = c.sb("r_ngm", [P, 1], F32)
            zz = c.sb("r_zz", [P, 1], F32)
            rc2 = c.sb("r_rc2", [P, 1], F32)
            rp1 = c.sb("r_rp1", [P, 1], F32)
            oh = c.sb("r_oh", [P, 8], F32)
            cnt_sb = c.sb("r_cnt", [8, 1], F32)
            c.dma_join()
            for k in range(NKD):
                i = nc.sync.dma_start(out=wswl[:, 8 * k:8 * k + 8],
                                      in_=dram["L1_ffn_Wsw_t"][k * P:(k + 1) * P, :])
                c.inc(i, "ld", 16)
            i = nc.sync.dma_start(out=bswr[:], in_=dram["L1_ffn_bsw_row"])
            c.inc(i, "ld", 16)
            c.wait(nc.vector, "ld")
            i = nc.vector.tensor_copy(wsw_bf[:], wswl[:])
            c.fin(i, "v", nc.vector)
            c.wait(nc.tensor, "v")
            plg = c.ps("r_plg", [P, 8])
            pt8 = c.ps("r_pt8", [8, P])
            pt1 = c.ps("r_pt1", [1, P])
            i = nc.tensor.matmul(plg[:], cst.ones_row[:], bswr[:],
                                 start=True, stop=True)
            c.inc(i, "t")
            c.wait(nc.vector, "t")
            i = nc.vector.tensor_copy(bswb[:], plg[:])
            c.fin(i, "v", nc.vector)
            c.wait(nc.tensor, "v")
            for tt in range(T // P):
                ts_ = slice(tt * P, (tt + 1) * P)
                for k in range(NKD):
                    i = nc.tensor.matmul(plg[:], hidT[k][:, ts_],
                                         wsw_bf[:, 8 * k:8 * k + 8],
                                         start=(k == 0), stop=(k == NKD - 1))
                c.inc(i, "t")
                c.wait(nc.vector, "t")
                c.sv(nc.vector.tensor_tensor(lg[:], plg[:], bswb[:], ALU.add))
                c.sv(nc.vector.reduce_max(mx[:, 0:1], lg[:], axis=AX.X))
                i = nc.vector.tensor_scalar_mul(ngm[:, 0:1], mx[:, 0:1], -1.0)
                c.fin(i, "v", nc.vector)
                c.wait(nc.scalar, "v")
                i = nc.scalar.activation(em[:], lg[:], AF.Exp, bias=ngm[:, 0:1],
                                         accum_out=zz[:, 0:1])
                c.fin(i, "s", nc.scalar)
                c.wait(nc.vector, "s")
                c.sv(nc.vector.reciprocal(rc2[:, 0:1], zz[:, 0:1]))
                c.sv(nc.vector.reduce_max(rp1[:, 0:1], em[:], axis=AX.X))
                c.sv(nc.vector.tensor_mul(rp1[:, 0:1], rp1[:, 0:1], rc2[:, 0:1]))
                c.sv(nc.vector.tensor_scalar_add(lg[:], lg[:], ngm[:, 0:1]))
                i = nc.vector.tensor_single_scalar(oh[:], lg[:], 0.0, ALU.is_ge)
                c.fin(i, "v", nc.vector)
                c.wait(nc.tensor, "v")
                nc.tensor.transpose(pt8[:], oh[:], cst.ident_f[:])
                i = nc.tensor.transpose(pt1[:], rp1[:, 0:1], cst.ident_f[:])
                c.inc(i, "t")
                c.wait(nc.vector, "t")
                c.sv(nc.vector.tensor_copy(ohT[:, ts_], pt8[:]))
                i = nc.vector.tensor_copy(rpT[0:1, ts_], pt1[:])
                c.fin(i, "v", nc.vector)
                c.wait(nc.tensor, "v")
            i = nc.vector.tensor_reduce(cnt_sb[:, 0:1], ohT[:], axis=AX.X,
                                        op=ALU.add)
            c.fin(i, "v", nc.vector)
            c.wait(nc.gpsimd, "v")
            for k in range(NKD):
                i = nc.gpsimd.dma_start(out=hid_out[k * P:(k + 1) * P, :],
                                        in_=hidT[k][:])
                c.inc(i, "st", 16)
            i = nc.gpsimd.dma_start(out=oh_out, in_=ohT[:])
            c.inc(i, "st", 16)
            i = nc.gpsimd.dma_start(out=rp_out, in_=rpT[:])
            c.inc(i, "st", 16)
            i = nc.gpsimd.dma_start(out=cnt_out, in_=cnt_sb[:])
            c.inc(i, "st", 16)


def emit_l1moe(c, cst, dram, x_in, emb_in, hid_in, oh_in, rp_in, ctbl_in,
               pmask_in, x_out, capacity=4915.0, dbg=None):
    nc = c.nc
    with c.scope():
        rpT = c.sb("m_rpT", [1, T], F32)
        ctbl = c.sb("m_ctbl", [16, 8], F32)
        pmask = c.sb("m_pm", [16, 1], F32)
        gam = c.sb("m_gam", [P, NKD], F32)
        bet = c.sb("m_bet", [P, NKD], F32)
        gam2 = c.sb("m_gam2", [P, NKD], F32)
        bet2 = c.sb("m_bet2", [P, NKD], F32)
        mdisp = c.sb("m_md", [8, T], F32)
        mdisp_bf = c.sb("m_mdb", [8, T], BF16)
        offs = c.sb("m_offs", [8, 1], F32)
        be_bf = c.sb("m_beb", [8, D], BF16)
        bel = c.sb("m_bel", [8, D], F32)
        hid = [c.sb("m_hid", [P, T], BF16) for _ in range(NKD)]
        yacc = [c.sb("m_y", [P, T], F32) for _ in range(NKD)]
        routing_scope = c.scope()
        routing_es = routing_scope.__enter__()
        ohT = c.sb("m_ohT", [8, T], F32)
        ones8 = c.sb("m_on8", [8, T], F32)
        pos = c.sb("m_pos", [8, T], F32)
        c.dma_join()
        if "st" in c.cnt:
            c.wait(nc.sync, "st")
        i = nc.sync.dma_start(out=ohT[:], in_=oh_in)
        c.inc(i, "ld", 16)
        i = nc.sync.dma_start(out=rpT[:], in_=rp_in)
        c.inc(i, "ld", 16)
        i = nc.sync.dma_start(out=ctbl[:], in_=ctbl_in)
        c.inc(i, "ld", 16)
        i = nc.sync.dma_start(out=pmask[:], in_=pmask_in)
        c.inc(i, "ld", 16)
        i = nc.sync.dma_start(out=bel[:], in_=dram["L1_ffn_be"])
        c.inc(i, "ld", 16)
        for k in range(NKD):
            i = nc.sync.dma_start(out=gam[:, k:k + 1], in_=dram["L1_ffn_gamma"][k])
            c.inc(i, "ld", 16)
            i = nc.sync.dma_start(out=bet[:, k:k + 1], in_=dram["L1_ffn_beta"][k])
            c.inc(i, "ld", 16)
        ldh = load_tiles(c, hid, hid_in, NKD, T)
        c.wait(nc.vector, "ld", ldh)
        c.sv(nc.vector.tensor_scalar_mul(gam2[:], gam[:], 0.5))
        c.sv(nc.vector.tensor_scalar_mul(bet2[:], bet[:], 0.5))
        c.sv(nc.vector.tensor_copy(be_bf[:], bel[:]))
        c.sv(nc.vector.memset(ones8[:], 1.0))
        i = nc.vector.engine_nop()
        c.fin(i, "v", nc.vector)
        poff = c.ps("m_poff", [8, 1])
        c.wait(nc.tensor, "v")
        i = nc.tensor.matmul(poff[:], ctbl[:], pmask[:], start=True, stop=True)
        c.inc(i, "t")
        c.wait(nc.vector, "t")
        c.sv(nc.vector.tensor_copy(offs[:], poff[:]))
        c.sv(nc.vector.tensor_tensor_scan(pos[:], ones8[:], ohT[:],
                                          offs[:, 0:1], ALU.mult, ALU.add))
        c.sv(nc.vector.tensor_single_scalar(pos[:], pos[:], capacity + 0.5,
                                            ALU.is_le))
        c.sv(nc.vector.tensor_mul(mdisp[:], pos[:], ohT[:]))
        i = nc.vector.tensor_copy(mdisp_bf[:], mdisp[:])
        c.fin(i, "v", nc.vector)
        routing_scope.__exit__(None, None, None)
        with c.scope():
            wl = [c.sb("m_wl", [P, D], F32) for _ in range(2)]
            we_bf = [c.sb("m_web", [P, D], BF16) for _ in range(NKD)]
            mh = [c.sb("m_mh", [P, FC], BF16) for _ in range(NKD)]
            pmb = c.ps("m_pmb", [P, FC])
            pyy = [c.ps("m_pyy", [P, FC]) for _ in range(2)]
            py_free = [0, 0]
            cd = []
            c.dma_join()
            for e in range(8):
                for k in range(NKD):
                    if len(cd) >= 2:
                        c.wait(nc.sync, "v", cd[-2])
                    i = nc.sync.dma_start(
                        out=wl[len(cd) % 2][:],
                        in_=dram["L1_ffn_We_t"][e, k * P:(k + 1) * P, :])
                    c.inc(i, "ld", 16)
                    c.wait(nc.vector, "ld")
                    i = nc.vector.tensor_copy(we_bf[k][:], wl[len(cd) % 2][:])
                    cd.append(c.inc(i, "v"))
                c.wait(nc.tensor, "v")
                for f in range(NFC):
                    fs = slice(f * FC, (f + 1) * FC)
                    i = nc.tensor.matmul(pmb[:], cst.sel8[:, e * P:(e + 1) * P],
                                         mdisp[:, fs], start=True, stop=True)
                    c.inc(i, "t")
                    c.wait(nc.vector, "t")
                    for k in range(NKD):
                        i = nc.vector.tensor_tensor(mh[k][:], hid[k][:, fs],
                                                    pmb[:], ALU.mult)
                    c.fin(i, "v", nc.vector)
                    c.wait(nc.tensor, "v")
                    for mo in range(NKD):
                        mos = slice(mo * P, (mo + 1) * P)
                        pyt = pyy[mo % 2]
                        c.wait(nc.tensor, "v", py_free[mo % 2])
                        for k in range(NKD):
                            i = nc.tensor.matmul(pyt[:], we_bf[k][:, mos],
                                                 mh[k][:], start=(k == 0),
                                                 stop=(e > 0 and k == NKD - 1))
                        if e == 0:
                            i = nc.tensor.matmul(pyt[:], be_bf[:, mos],
                                                 mdisp_bf[:, fs],
                                                 start=False, stop=True)
                        c.inc(i, "t")
                        c.wait(nc.vector, "t")
                        if e == 0:
                            i = nc.vector.tensor_copy(yacc[mo][:, fs], pyt[:])
                        else:
                            i = nc.vector.tensor_add(yacc[mo][:, fs],
                                                     yacc[mo][:, fs], pyt[:])
                        py_free[mo % 2] = c.fin_val(i, "v", nc.vector)
                    c.wait(nc.tensor, "v")
        if dbg:
            c.wait(nc.gpsimd, "v")
            for k in range(NKD):
                i = nc.gpsimd.dma_start(out=dbg["yacc"][k * P:(k + 1) * P, :],
                                        in_=yacc[k][:])
                c.inc(i, "st", 16)
            i = nc.gpsimd.dma_start(out=dbg["md"], in_=mdisp[:])
            c.inc(i, "st", 16)
            c.wait(nc.tensor, "st")
            c.wait(nc.vector, "st")
        FE = 256
        with c.scope():
            notk = c.sb("m_ntk", [P, FE], F32)
            tmpa = c.sb("m_tma", [P, FE], F32)
            x2 = [c.sb("m_x2", [P, FE], F32) for _ in range(NKD)]
            emb = [c.sb("m_emb", [P, FE], F32) for _ in range(NKD)]
            x3 = [c.sb("m_x3", [P, FE], F32) for _ in range(NKD)]
            lno = emb
            pkb = c.ps("m_pkb", [P, FE])
            prp = c.ps("m_prp", [P, FE])
            for f in range(T // FE):
                fs = slice(f * FE, (f + 1) * FE)
                c.dma_join()
                if "st" in c.cnt:
                    c.wait(nc.sync, "st")
                for k in range(NKD):
                    i = nc.sync.dma_start(out=emb[k][:],
                                          in_=emb_in[k * P:(k + 1) * P, fs])
                    c.inc(i, "ld", 16)
                    i = nc.sync.dma_start(out=x3[k][:],
                                          in_=x_in[k * P:(k + 1) * P, fs])
                    c.inc(i, "ld", 16)
                ldc = c.cnt["ld"]
                c.wait(nc.tensor, "ld", ldc)
                nc.tensor.matmul(pkb[:], cst.allon8[:], mdisp[:, fs],
                                 start=True, stop=True)
                i = nc.tensor.matmul(prp[:], cst.ones_row[:], rpT[0:1, fs],
                                     start=True, stop=True)
                c.inc(i, "t")
                for eng in (nc.vector, nc.scalar):
                    c.wait(eng, "ld", ldc)
                c.wait(nc.vector, "t")
                c.sv(nc.vector.tensor_scalar(notk[:], pkb[:], -1.0, 1.0,
                                             ALU.mult, ALU.add))
                for k in range(NKD):
                    c.sv(nc.vector.tensor_tensor(tmpa[:], hid[k][:, fs],
                                                 notk[:], ALU.mult))
                    c.sv(nc.vector.tensor_add(tmpa[:], tmpa[:], yacc[k][:, fs]))
                    c.sv(nc.vector.tensor_tensor(tmpa[:], tmpa[:], prp[:],
                                                 ALU.mult))
                    i = nc.vector.tensor_add(x2[k][:], tmpa[:], emb[k][:])
                c.fin(i, "v", nc.vector)
                c.wait(nc.scalar, "v")
                c.wait(nc.tensor, "v")
                emit_ln(c, cst, x2, None, FE,
                        gamma=[gam2[:, k:k + 1] for k in range(NKD)],
                        beta=[bet2[:, k:k + 1] for k in range(NKD)],
                        out_f32=lno)
                for k in range(NKD):
                    i = nc.vector.tensor_add(lno[k][:], lno[k][:], x3[k][:])
                c.fin(i, "v", nc.vector)
                c.wait(nc.gpsimd, "v")
                for k in range(NKD):
                    i = nc.gpsimd.dma_start(out=x_out[k * P:(k + 1) * P, fs],
                                            in_=lno[k][:])
                    c.inc(i, "st", 16)


def emit_l1attn(c, cst, dram, x_in, x_out, dbg=None):
    nc = c.nc
    H, DH = 8, 128
    with c.scope():
        xnb = [c.sb("a_xn", [P, T], BF16) for _ in range(NKD)]
        xnl = [c.sb("a_xnl", [P, T], BF16) for _ in range(NKD)]
        bq = c.sb("a_bq", [P, H], F32)
        bo = c.sb("a_bo", [P, NKD], F32)
        bkr = c.sb("a_bkr", [1, P * H], F32)
        bvr = c.sb("a_bvr", [1, P * H], F32)
        with c.scope():
            x_sb = [c.sb("a_x", [P, T], F32) for _ in range(NKD)]
            c.dma_join()
            if "st" in c.cnt:
                c.wait(nc.sync, "st")
            ldx = load_tiles(c, x_sb, x_in, NKD, T)
            for h in range(H):
                i = nc.sync.dma_start(out=bq[:, h:h + 1], in_=dram["L1_attn_bq"][h])
                c.inc(i, "ld", 16)
                i = nc.sync.dma_start(out=bkr[:, h * P:(h + 1) * P],
                                      in_=dram["L1_attn_bk_row"][h])
                c.inc(i, "ld", 16)
                i = nc.sync.dma_start(out=bvr[:, h * P:(h + 1) * P],
                                      in_=dram["L1_attn_bv_row"][h])
                c.inc(i, "ld", 16)
            for k in range(NKD):
                i = nc.sync.dma_start(out=bo[:, k:k + 1], in_=dram["L1_attn_bo"][k])
                c.inc(i, "ld", 16)
            for eng in (nc.tensor, nc.vector, nc.scalar):
                c.wait(eng, "ld")
            emit_ln(c, cst, x_sb, xnb, T, out_lo=xnl)
        outTh = [c.sb("a_oTh", [P, T], BF16) for _ in range(NKD)]
        outTl = [c.sb("a_oTl", [P, T], BF16) for _ in range(NKD)]
        with c.scope():
            qTh = c.sb("a_qTh", [P, T], BF16)
            qTl = c.sb("a_qTl", [P, T], BF16)
            Kh = [c.sb("a_Kh", [P, DH], BF16) for _ in range(T // P)]
            Kl = [c.sb("a_Kl", [P, DH], BF16) for _ in range(T // P)]
            Vh = [c.sb("a_Vh", [P, DH], BF16) for _ in range(T // P)]
            Vl = [c.sb("a_Vl", [P, DH], BF16) for _ in range(T // P)]
            Sh = c.sb("a_Sh", [P, P], BF16)
            Sl = c.sb("a_Sl", [P, P], BF16)
            bkb = c.sb("a_bkb", [P, P], F32)
            bvb = c.sb("a_bvb", [P, P], F32)
            tf = c.sb("a_tf", [P, FC], F32)
            tu = c.sb("a_tu", [P, FC], F32)
            wl = [c.sb("a_wl", [P, DH], F32) for _ in range(2)]
            wqh = [c.sb("a_wqh", [P, DH], BF16) for _ in range(NKD)]
            wql = [c.sb("a_wql", [P, DH], BF16) for _ in range(NKD)]
            wkh = [c.sb("a_wkh", [P, DH], BF16) for _ in range(NKD)]
            wkl = [c.sb("a_wkl", [P, DH], BF16) for _ in range(NKD)]
            wvh = [c.sb("a_wvh", [P, DH], BF16) for _ in range(NKD)]
            wvl = [c.sb("a_wvl", [P, DH], BF16) for _ in range(NKD)]
            cd = []

            def stream(hi_list, lo_list, src3):
                for k in range(NKD):
                    if len(cd) >= 2:
                        c.wait(nc.sync, "v", cd[-2])
                    i = nc.sync.dma_start(out=wl[len(cd) % 2][:],
                                          in_=src3[k * P:(k + 1) * P, :])
                    c.inc(i, "ld", 16)
                    c.wait(nc.vector, "ld")
                    c.sv(nc.vector.tensor_copy(hi_list[k][:],
                                               wl[len(cd) % 2][:]))
                    c.sv(nc.vector.tensor_copy(tu[:, 0:DH], hi_list[k][:]))
                    c.sv(nc.vector.tensor_sub(tu[:, 0:DH], wl[len(cd) % 2][:],
                                              tu[:, 0:DH]))
                    i = nc.vector.tensor_copy(lo_list[k][:], tu[:, 0:DH])
                    cd.append(c.inc(i, "v"))

            def split_evac(pf_ap, hi_ap, lo_ap, flen):
                c.sv(nc.vector.tensor_copy(hi_ap, tf[:, 0:flen]))
                c.sv(nc.vector.tensor_copy(tu[:, 0:flen], hi_ap))
                c.sv(nc.vector.tensor_sub(tu[:, 0:flen], tf[:, 0:flen],
                                          tu[:, 0:flen]))
                i = nc.vector.tensor_copy(lo_ap, tu[:, 0:flen])
                return i
            c.dma_join()
            for h in range(H):
                with c.scope():
                    pq = c.ps("a_pq", [P, FC])
                    pkv = c.ps("a_pkv", [P, DH])
                    pS = c.ps("a_pS", [P, P])
                    stream(wqh, wql, dram["L1_attn_wq"][h])
                    stream(wkh, wkl, dram["L1_attn_wk"][h])
                    stream(wvh, wvl, dram["L1_attn_wv"][h])
                    c.wait(nc.tensor, "v")
                    for f in range(NFC):
                        fs = slice(f * FC, (f + 1) * FC)
                        for k in range(NKD):
                            nc.tensor.matmul(pq[:], wqh[k][:], xnb[k][:, fs],
                                             start=(k == 0), stop=False)
                            nc.tensor.matmul(pq[:], wql[k][:], xnb[k][:, fs],
                                             start=False, stop=False)
                            i = nc.tensor.matmul(pq[:], wqh[k][:],
                                                 xnl[k][:, fs], start=False,
                                                 stop=(k == NKD - 1))
                        c.inc(i, "t")
                        c.wait(nc.scalar, "t")
                        i = nc.scalar.activation(tf[:], pq[:], AF.Identity,
                                                 bias=bq[:, h:h + 1])
                        c.fin(i, "s", nc.scalar)
                        c.wait(nc.vector, "s")
                        i = split_evac(None, qTh[:, fs], qTl[:, fs], FC)
                        c.fin(i, "v", nc.vector)
                        c.wait(nc.tensor, "v")
                    nc.tensor.matmul(pq[:, 0:P], cst.ones_row[:],
                                     bkr[0:1, h * P:(h + 1) * P],
                                     start=True, stop=True)
                    i = nc.tensor.matmul(pkv[:], cst.ones_row[:],
                                         bvr[0:1, h * P:(h + 1) * P],
                                         start=True, stop=True)
                    c.inc(i, "t")
                    c.wait(nc.vector, "t")
                    c.sv(nc.vector.tensor_copy(bkb[:], pq[:, 0:P]))
                    i = nc.vector.tensor_copy(bvb[:], pkv[:])
                    c.fin(i, "v", nc.vector)
                    c.wait(nc.tensor, "v")
                    for tc in range(T // P):
                        tcs = slice(tc * P, (tc + 1) * P)
                        for which, wh_, wl_, Kh_, Kl_, bb in (
                                (0, wkh, wkl, Kh, Kl, bkb),
                                (1, wvh, wvl, Vh, Vl, bvb)):
                            for k in range(NKD):
                                nc.tensor.matmul(pkv[:], xnb[k][:, tcs],
                                                 wh_[k][:], start=(k == 0),
                                                 stop=False)
                                nc.tensor.matmul(pkv[:], xnl[k][:, tcs],
                                                 wh_[k][:], start=False,
                                                 stop=False)
                                i = nc.tensor.matmul(pkv[:], xnb[k][:, tcs],
                                                     wl_[k][:], start=False,
                                                     stop=(k == NKD - 1))
                            c.inc(i, "t")
                            c.wait(nc.vector, "t")
                            i = nc.vector.tensor_tensor(tf[:, 0:DH], pkv[:],
                                                        bb[:], ALU.add)
                            c.sv(i)
                            i = split_evac(None, Kh_[tc][:], Kl_[tc][:], DH)
                            c.fin(i, "v", nc.vector)
                            c.wait(nc.tensor, "v")
                    for tc in range(T // P):
                        nc.tensor.matmul(pS[:], Kh[tc][:], Vh[tc][:],
                                         start=(tc == 0), stop=False)
                        nc.tensor.matmul(pS[:], Kl[tc][:], Vh[tc][:],
                                         start=False, stop=False)
                        i = nc.tensor.matmul(pS[:], Kh[tc][:], Vl[tc][:],
                                             start=False,
                                             stop=(tc == T // P - 1))
                    c.inc(i, "t")
                    c.wait(nc.vector, "t")
                    c.sv(nc.vector.tensor_copy(tf[:, 0:P], pS[:]))
                    i = split_evac(None, Sh[:], Sl[:], P)
                    c.fin(i, "v", nc.vector)
                    c.wait(nc.tensor, "v")
                    if dbg and h == 0:
                        c.wait(nc.gpsimd, "v")
                        i = nc.gpsimd.dma_start(out=dbg["S0"], in_=Sh[:])
                        c.inc(i, "st", 16)
                        i = nc.gpsimd.dma_start(out=dbg["qT0"], in_=qTh[:])
                        c.inc(i, "st", 16)
                        i = nc.gpsimd.dma_start(out=dbg["K0"], in_=Kh[0][:])
                        c.inc(i, "st", 16)
                        c.wait(nc.tensor, "st")
                    for f in range(NFC):
                        fs = slice(f * FC, (f + 1) * FC)
                        nc.tensor.matmul(pq[:], Sh[:], qTh[:, fs],
                                         start=True, stop=False)
                        nc.tensor.matmul(pq[:], Sl[:], qTh[:, fs],
                                         start=False, stop=False)
                        i = nc.tensor.matmul(pq[:], Sh[:], qTl[:, fs],
                                             start=False, stop=True)
                        c.inc(i, "t")
                        c.wait(nc.vector, "t")
                        c.sv(nc.vector.tensor_copy(tf[:], pq[:]))
                        i = split_evac(None, outTh[h][:, fs], outTl[h][:, fs],
                                       FC)
                        c.fin(i, "v", nc.vector)
                        c.wait(nc.tensor, "v")
        with c.scope():
            wl2 = [c.sb("a_wl2", [P, D], F32) for _ in range(2)]
            woh = [c.sb("a_woh", [P, D], BF16) for _ in range(NKD)]
            wol = [c.sb("a_wol", [P, D], BF16) for _ in range(NKD)]
            tmpo = c.sb("a_tmpo", [P, FC], F32)
            tupo = c.sb("a_tupo", [P, D], F32)
            xres = c.sb("a_xres", [P, FC], F32)
            cd2 = []
            c.dma_join()
            for k in range(NKD):
                if len(cd2) >= 2:
                    c.wait(nc.sync, "v", cd2[-2])
                i = nc.sync.dma_start(out=wl2[len(cd2) % 2][:],
                                      in_=dram["L1_attn_wo_t"][k * P:(k + 1) * P, :])
                c.inc(i, "ld", 16)
                c.wait(nc.vector, "ld")
                c.sv(nc.vector.tensor_copy(woh[k][:], wl2[len(cd2) % 2][:]))
                c.sv(nc.vector.tensor_copy(tupo[:], woh[k][:]))
                c.sv(nc.vector.tensor_sub(tupo[:], wl2[len(cd2) % 2][:],
                                          tupo[:]))
                i = nc.vector.tensor_copy(wol[k][:], tupo[:])
                cd2.append(c.inc(i, "v"))
            c.wait(nc.tensor, "v")
            po = c.ps("a_po", [P, FC])
            for f in range(NFC):
                fs = slice(f * FC, (f + 1) * FC)
                for mo in range(NKD):
                    mos = slice(mo * P, (mo + 1) * P)
                    c.dma_join(names=("v",))
                    i = nc.sync.dma_start(out=xres[:],
                                          in_=x_in[mo * P:(mo + 1) * P, fs])
                    c.inc(i, "ld", 16)
                    for k in range(NKD):
                        nc.tensor.matmul(po[:], woh[k][:, mos], outTh[k][:, fs],
                                         start=(k == 0), stop=False)
                        nc.tensor.matmul(po[:], wol[k][:, mos], outTh[k][:, fs],
                                         start=False, stop=False)
                        i = nc.tensor.matmul(po[:], woh[k][:, mos],
                                             outTl[k][:, fs], start=False,
                                             stop=(k == NKD - 1))
                    c.inc(i, "t")
                    c.wait(nc.scalar, "t")
                    i = nc.scalar.activation(tmpo[:], po[:], AF.Identity,
                                             bias=bo[:, mo:mo + 1])
                    c.fin(i, "s", nc.scalar)
                    c.wait(nc.vector, "s")
                    c.wait(nc.vector, "ld")
                    i = nc.vector.tensor_add(xres[:], xres[:], tmpo[:])
                    c.fin(i, "v", nc.vector)
                    c.wait(nc.gpsimd, "v")
                    i = nc.gpsimd.dma_start(out=x_out[mo * P:(mo + 1) * P, fs],
                                            in_=xres[:])
                    c.inc(i, "st", 16)
                    c.wait(nc.tensor, "v")
                    c.wait(nc.sync, "st")
"""Driver appended to mk.py to form kernel.py (multi-launch version)."""

from concourse.bass_utils import run_bass_kernel_spmd

NB = 2
NCORE = 8

_CONSTS = const_inputs()


def _cin_specs():
    return {k: (v.shape, BF16 if v.dtype.name == "bfloat16" else F32)
            for k, v in _CONSTS.items()}


_XS = ((NB, 1024, 2048), F32)


def _prog_ffn0():
    ins = {
        "xin": _XS,
        "L0_ffn_w_gate": ((1024, 8), F32),
        "L0_ffn_w_g": ((8, 1024, 1408), F32),
        "L0_ffn_w_u": ((8, 1024, 1408), F32),
        "L0_ffn_w_d": ((8, 1408, 1024), F32),
    }
    ins.update(_cin_specs())
    outs = {"xout": _XS}

    def phase(c, cst, dram):
        for b in range(NB):
            emit_ffn0(c, cst, dram, dram["xin"][b], dram["xout"][b])
    return build_program([phase], ins, outs, consts_need=("sel8", "ident_f"))


def _prog_mamba():
    ins = {
        "xin": _XS,
        "L0_attn_ln_g": ((8, 128, 1), F32),
        "L0_attn_ln_b": ((8, 128, 1), F32),
        "L0_attn_W_in": ((1024, 4096), F32),
        "L0_attn_conv_w": ((16, 128, 4), F32),
        "L0_attn_conv_b": ((16, 128, 1), F32),
        "L0_attn_W_b": ((2048, 16), F32),
        "L0_attn_W_c": ((2048, 16), F32),
        "L0_attn_W_delta": ((2048, 2048), F32),
        "L0_attn_b_delta": ((16, 128, 1), F32),
        "L0_attn_log_A": ((16, 128, 16), F32),
        "L0_attn_Dp": ((16, 128, 1), F32),
        "L0_attn_W_out": ((2048, 1024), F32),
    }
    ins.update(_cin_specs())
    outs = {"xout": _XS}

    def phase(c, cst, dram):
        for b in range(NB):
            emit_mamba(c, cst, dram, dram["xin"][b], dram["xout"][b])
    return build_program([phase], ins, outs, consts_need=("sel16",))


def _prog_router():
    ins = {
        "xin": _XS,
        "L1_ffn_Wq": ((16, 1024, 64), F32),
        "L1_ffn_Wk": ((16, 1024, 64), F32),
        "L1_ffn_Wv": ((16, 1024, 64), F32),
        "L1_ffn_bq": ((16, 64, 1), F32),
        "L1_ffn_bk": ((16, 64, 1), F32),
        "L1_ffn_bv": ((16, 1, 64), F32),
        "L1_ffn_Wsw_t": ((1024, 8), F32),
        "L1_ffn_bsw_row": ((1, 8), F32),
    }
    ins.update(_cin_specs())
    outs = {"hid": ((NB, 1024, 2048), BF16), "emb": _XS,
            "oh": ((NB, 8, 2048), F32), "rp": ((NB, 1, 2048), F32),
            "cnt": ((NB, 8, 1), F32)}

    def phase(c, cst, dram):
        for b in range(NB):
            emit_l1router(c, cst, dram, dram["xin"][b], dram["hid"][b],
                          dram["emb"][b], dram["oh"][b], dram["rp"][b],
                          dram["cnt"][b])
    return build_program([phase], ins, outs,
                         consts_need=("ident_bf", "ident_f", "tril_f"))


def _prog_moe():
    ins = {
        "xin": _XS,
        "emb": _XS,
        "hid": ((NB, 1024, 2048), BF16),
        "oh": ((NB, 8, 2048), F32),
        "rp": ((NB, 1, 2048), F32),
        "ctbl": ((16, 8), F32),
        "pmask": ((NB, 16, 1), F32),
        "L1_ffn_We_t": ((8, 1024, 1024), F32),
        "L1_ffn_be": ((8, 1024), F32),
        "L1_ffn_gamma": ((8, 128, 1), F32),
        "L1_ffn_beta": ((8, 128, 1), F32),
    }
    ins.update(_cin_specs())
    outs = {"xout": _XS}

    def phase(c, cst, dram):
        for b in range(NB):
            emit_l1moe(c, cst, dram, dram["xin"][b], dram["emb"][b],
                       dram["hid"][b], dram["oh"][b], dram["rp"][b],
                       dram["ctbl"], dram["pmask"][b], dram["xout"][b])
    return build_program([phase], ins, outs, consts_need=("sel8", "allon8"))


def _prog_attn():
    ins = {
        "xin": _XS,
        "L1_attn_wq": ((8, 1024, 128), F32),
        "L1_attn_wk": ((8, 1024, 128), F32),
        "L1_attn_wv": ((8, 1024, 128), F32),
        "L1_attn_bq": ((8, 128, 1), F32),
        "L1_attn_bk_row": ((8, 1, 128), F32),
        "L1_attn_bv_row": ((8, 1, 128), F32),
        "L1_attn_wo_t": ((1024, 1024), F32),
        "L1_attn_bo": ((8, 128, 1), F32),
    }
    ins.update(_cin_specs())
    outs = {"xout": _XS}

    def phase(c, cst, dram):
        for b in range(NB):
            emit_l1attn(c, cst, dram, dram["xin"][b], dram["xout"][b])
    return build_program([phase], ins, outs, consts_need=())


_PROGS = {}


def _prog(name):
    if name not in _PROGS:
        _PROGS[name] = {"ffn0": _prog_ffn0, "mamba": _prog_mamba,
                        "router": _prog_router, "moe": _prog_moe,
                        "attn": _prog_attn}[name]()
    return _PROGS[name]


def _layout_weights(inp):
    W = {}
    W["L0_ffn_w_gate"] = np.ascontiguousarray(inp["L0_ffn_w_gate"])
    for k in ("L0_ffn_w_g", "L0_ffn_w_u", "L0_ffn_w_d"):
        W[k] = np.ascontiguousarray(inp[k])
    W["L0_attn_ln_g"] = np.ascontiguousarray(inp["L0_attn_ln_g"]).reshape(8, 128, 1)
    W["L0_attn_ln_b"] = np.ascontiguousarray(inp["L0_attn_ln_b"]).reshape(8, 128, 1)
    W["L0_attn_W_in"] = np.ascontiguousarray(inp["L0_attn_W_in"])
    W["L0_attn_conv_w"] = np.ascontiguousarray(inp["L0_attn_conv_w"]).reshape(16, 128, 4)
    W["L0_attn_conv_b"] = np.ascontiguousarray(inp["L0_attn_conv_b"]).reshape(16, 128, 1)
    W["L0_attn_W_b"] = np.ascontiguousarray(inp["L0_attn_W_b"])
    W["L0_attn_W_c"] = np.ascontiguousarray(inp["L0_attn_W_c"])
    W["L0_attn_W_delta"] = np.ascontiguousarray(inp["L0_attn_W_delta"])
    W["L0_attn_b_delta"] = np.ascontiguousarray(inp["L0_attn_b_delta"]).reshape(16, 128, 1)
    W["L0_attn_log_A"] = np.ascontiguousarray(inp["L0_attn_log_A"]).reshape(16, 128, 16)
    W["L0_attn_Dp"] = np.ascontiguousarray(inp["L0_attn_Dp"]).reshape(16, 128, 1)
    W["L0_attn_W_out"] = np.ascontiguousarray(inp["L0_attn_W_out"])
    W["L1_ffn_Wq"] = np.ascontiguousarray(inp["L1_ffn_Wq"].transpose(0, 2, 1))
    W["L1_ffn_Wk"] = np.ascontiguousarray(inp["L1_ffn_Wk"].transpose(0, 2, 1))
    W["L1_ffn_Wv"] = np.ascontiguousarray(inp["L1_ffn_Wv"].transpose(0, 2, 1))
    W["L1_ffn_bq"] = np.ascontiguousarray(inp["L1_ffn_bq"])[:, :, None]
    W["L1_ffn_bk"] = np.ascontiguousarray(inp["L1_ffn_bk"])[:, :, None]
    W["L1_ffn_bv"] = np.ascontiguousarray(inp["L1_ffn_bv"])[:, None, :]
    W["L1_ffn_Wsw_t"] = np.ascontiguousarray(inp["L1_ffn_Wsw"].T)
    W["L1_ffn_bsw_row"] = np.ascontiguousarray(inp["L1_ffn_bsw"])[None, :]
    W["L1_ffn_We_t"] = np.ascontiguousarray(inp["L1_ffn_We"].transpose(0, 2, 1))
    W["L1_ffn_be"] = np.ascontiguousarray(inp["L1_ffn_be"])
    W["L1_ffn_gamma"] = np.ascontiguousarray(inp["L1_ffn_gamma"]).reshape(8, 128, 1)
    W["L1_ffn_beta"] = np.ascontiguousarray(inp["L1_ffn_beta"]).reshape(8, 128, 1)
    w_qkv = np.asarray(inp["L1_attn_w_qkv"])
    b_qkv = np.asarray(inp["L1_attn_b_qkv"])
    wq_s = np.empty((8, 1024, 128), np.float32)
    wk_s = np.empty((8, 1024, 128), np.float32)
    wv_s = np.empty((8, 1024, 128), np.float32)
    bq_s = np.empty((8, 128, 1), np.float32)
    bk_r = np.empty((8, 1, 128), np.float32)
    bv_r = np.empty((8, 1, 128), np.float32)
    for h in range(8):
        rows = np.arange(128) * 8 + h
        wq_s[h] = w_qkv[rows].T
        wk_s[h] = w_qkv[1024 + rows].T
        wv_s[h] = w_qkv[2048 + rows].T
        bq_s[h, :, 0] = b_qkv[rows]
        bk_r[h, 0] = b_qkv[1024 + rows]
        bv_r[h, 0] = b_qkv[2048 + rows]
    W["L1_attn_wq"], W["L1_attn_wk"], W["L1_attn_wv"] = wq_s, wk_s, wv_s
    W["L1_attn_bq"], W["L1_attn_bk_row"], W["L1_attn_bv_row"] = bq_s, bk_r, bv_r
    W["L1_attn_wo_t"] = np.ascontiguousarray(np.asarray(inp["L1_attn_w_o"]).T)
    W["L1_attn_bo"] = np.ascontiguousarray(inp["L1_attn_b_o"]).reshape(8, 128, 1)
    W.update(_CONSTS)
    return W


def _launch(name, per_core):
    nc = _prog(name)
    res = run_bass_kernel_spmd(nc, per_core, core_ids=list(range(NCORE)))
    return [r for r in res.results]


def kernel(**inputs):
    W = _layout_weights(inputs)
    x = np.ascontiguousarray(
        np.asarray(inputs["L0_attn_x"]).transpose(0, 2, 1)).astype(np.float32)
    shards = [np.ascontiguousarray(x[2 * i:2 * i + 2]) for i in range(NCORE)]

    def maps(extra_keys, per_core_extra=None):
        out = []
        for i in range(NCORE):
            m = {k: W[k] for k in extra_keys if k in W}
            m.update(_CONSTS)
            if per_core_extra is not None:
                m.update(per_core_extra[i])
            out.append(m)
        return out

    fk = ("L0_ffn_w_gate", "L0_ffn_w_g", "L0_ffn_w_u", "L0_ffn_w_d")
    mk_ = tuple(k for k in W if k.startswith("L0_attn_"))
    rk = ("L1_ffn_Wq", "L1_ffn_Wk", "L1_ffn_Wv", "L1_ffn_bq", "L1_ffn_bk",
          "L1_ffn_bv", "L1_ffn_Wsw_t", "L1_ffn_bsw_row")
    mok = ("L1_ffn_We_t", "L1_ffn_be", "L1_ffn_gamma", "L1_ffn_beta")
    ak = tuple(k for k in W if k.startswith("L1_attn_"))

    r = _launch("ffn0", maps(fk, [{"xin": shards[i]} for i in range(NCORE)]))
    x1 = [r[i]["xout"] for i in range(NCORE)]
    r = _launch("mamba", maps(mk_, [{"xin": x1[i]} for i in range(NCORE)]))
    x2 = [r[i]["xout"] for i in range(NCORE)]
    r = _launch("ffn0", maps(fk, [{"xin": x2[i]} for i in range(NCORE)]))
    x3 = [r[i]["xout"] for i in range(NCORE)]

    def l1ffn(xc):
        r = _launch("router", maps(rk, [{"xin": xc[i]} for i in range(NCORE)]))
        ctbl = np.concatenate([r[i]["cnt"][:, :, 0] for i in range(NCORE)],
                              axis=0).astype(np.float32)
        pm = np.tril(np.ones((16, 16), np.float32), -1)
        per = []
        for i in range(NCORE):
            per.append({
                "xin": xc[i], "emb": r[i]["emb"], "hid": r[i]["hid"],
                "oh": r[i]["oh"], "rp": r[i]["rp"],
                "ctbl": np.ascontiguousarray(ctbl),
                "pmask": np.ascontiguousarray(
                    pm[[2 * i, 2 * i + 1]][:, :, None]),
            })
        r2 = _launch("moe", maps(mok, per))
        return [r2[i]["xout"] for i in range(NCORE)]

    x4 = l1ffn(x3)
    r = _launch("attn", maps(ak, [{"xin": x4[i]} for i in range(NCORE)]))
    x5 = [r[i]["xout"] for i in range(NCORE)]
    x6 = l1ffn(x5)

    full = np.concatenate(x6, axis=0)
    return np.ascontiguousarray(full.transpose(0, 2, 1)).astype(np.float32)


# version 09
# speedup vs baseline: 1.0550x; 1.0550x over previous
import numpy as np
from contextlib import ExitStack
import concourse.bass as bass
import concourse.mybir as mybir

F32 = mybir.dt.float32
BF16 = mybir.dt.bfloat16
AF = mybir.ActivationFunctionType
ALU = mybir.AluOpType
AX = mybir.AxisListType

P = 128
T = 2048
D = 1024
NKD = D // P
FC = 512
NFC = T // FC


class C:

    def __init__(self, nc):
        self.nc = nc
        self.es = ExitStack()
        self._stack = [self.es]
        self._sems = {}
        self.cnt = {}
        self.uid = 0

    def u(self, pfx):
        self.uid += 1
        return f"{pfx}_{self.uid}"

    def scope(self):
        class _S:
            def __enter__(s):
                es = ExitStack()
                self._stack.append(es)
                return es

            def __exit__(s, *a):
                self._stack.pop().close()
                return False
        return _S()

    def sb(self, name, shape, dt):
        return self._stack[-1].enter_context(
            self.nc.sbuf_tensor(self.u(name), shape, dt))

    def ps(self, name, shape, dt=F32):
        return self._stack[-1].enter_context(
            self.nc.psum_tensor(self.u(name), shape, dt))

    def sem(self, name):
        if name not in self._sems:
            self._sems[name] = self.es.enter_context(self.nc.semaphore(name))
            self.cnt[name] = 0
        return self._sems[name]

    def inc(self, instr, name, k=1):
        s = self.sem(name)
        instr.then_inc(s, k)
        self.cnt[name] += k
        return self.cnt[name]

    def wait(self, eng, name, val=None):
        eng.wait_ge(self.sem(name), self.cnt[name] if val is None else val)

    def fin(self, i, name, eng):
        n = self.inc(i, name)
        eng.wait_ge(self.sem(name), n)
        return i

    def fin_val(self, i, name, eng):
        self.fin(i, name, eng)
        return self.cnt[name]

    def sv(self, i):
        n = self.inc(i, "ssv")
        self.nc.vector.wait_ge(self.sem("ssv"), n)
        return i

    def ssc(self, i):
        n = self.inc(i, "sss")
        self.nc.scalar.wait_ge(self.sem("sss"), n)
        return i

    def sg(self, i):
        n = self.inc(i, "ssg")
        self.nc.gpsimd.wait_ge(self.sem("ssg"), n)
        return i

    def phase_join(self):
        if "st" in self.cnt and self.cnt["st"] > 0:
            for eng in (self.nc.sync, self.nc.vector, self.nc.scalar,
                        self.nc.tensor, self.nc.gpsimd):
                self.wait(eng, "st")

    def dma_join(self, q=None, names=("t", "v", "s")):
        q = q or self.nc.sync
        for n in names:
            if n in self.cnt and self.cnt[n] > 0:
                self.wait(q, n)

    def close(self):
        self.es.close()


class Consts:

    def __init__(self, c, dram, need=("ident_bf", "ident_f", "tril_f", "sel8",
                                      "sel16", "allon8")):
        nc = c.nc
        shapes = {"ones_col": ([P, 1], F32), "ones_row": ([1, P], F32),
                  "ones_col_bf": ([P, 1], BF16),
                  "ident_bf": ([P, P], BF16), "ident_f": ([P, P], F32),
                  "tril_f": ([P, P], F32), "sel8": ([8, 8 * P], F32),
                  "sel16": ([16, 16 * P], F32), "allon8": ([8, P], F32)}
        need = set(need) | {"ones_col", "ones_row", "ones_col_bf"}
        for nm, (shp, dt) in shapes.items():
            if nm in need:
                t = c.sb(nm, shp, dt)
                setattr(self, nm, t)
                i = nc.sync.dma_start(out=t[:], in_=dram["c_" + nm])
                c.inc(i, "ld", 16)
            else:
                setattr(self, nm, None)
        self.ld_target = c.cnt["ld"]
        self.kone = c.sb("kone", [P, 1], F32)
        i = nc.vector.memset(self.kone[:], 1.0)
        c.fin(i, "v", nc.vector)

    def wait_all(self, c):
        for eng in (c.nc.tensor, c.nc.vector, c.nc.scalar, c.nc.gpsimd):
            c.wait(eng, "ld", self.ld_target)
        c.wait(c.nc.scalar, "v")
        c.wait(c.nc.tensor, "v")


def const_inputs():
    import ml_dtypes
    d = {}
    d["c_ones_col"] = np.ones((P, 1), np.float32)
    d["c_ones_col_bf"] = np.ones((P, 1), ml_dtypes.bfloat16)
    d["c_ones_row"] = np.ones((1, P), np.float32)
    d["c_ident_bf"] = np.eye(P, dtype=ml_dtypes.bfloat16)
    d["c_ident_f"] = np.eye(P, dtype=np.float32)
    d["c_tril_f"] = np.tril(np.ones((P, P), np.float32))
    sel = np.zeros((8, 8 * P), np.float32)
    for e in range(8):
        sel[e, e * P:(e + 1) * P] = 1.0
    d["c_sel8"] = sel
    sel = np.zeros((16, 16 * P), np.float32)
    for s in range(16):
        sel[s, s * P:(s + 1) * P] = 1.0
    d["c_sel16"] = sel
    d["c_allon8"] = np.ones((8, P), np.float32)
    return d


def load_tiles(c, dst_list, src2d, ntile, width, sem="ld", q=None):
    q = q or c.nc.sync
    for k in range(ntile):
        i = q.dma_start(out=dst_list[k][:, :width],
                        in_=src2d[k * P:(k + 1) * P, :width])
        c.inc(i, sem, 16)
    return c.cnt[sem]


def emit_ln(c, cst, x_tiles, out_bf, t_len, gamma=None, beta=None, out_f32=None,
            out_lo=None, eps=1e-5, nkd=NKD):
    nc = c.nc
    dmodel = nkd * P
    with c.scope():
        sq = [c.sb("lnsq", [P, FC], F32) for _ in range(nkd)]
        s1p = c.ps("lns1", [1, FC])
        s2p = c.ps("lns2", [1, FC])
        mrow = c.sb("lnm", [1, FC], F32)
        m2row = c.sb("lnm2", [1, FC], F32)
        vrow = c.sb("lnv", [1, FC], F32)
        rrow = c.sb("lnr", [1, FC], F32)
        mb = c.ps("lnmb", [P, FC])
        rb = c.ps("lnrb", [P, FC])
        tmp = c.sb("lntmp", [P, FC], F32)
        tmp2 = c.sb("lntmp2", [P, FC], F32)
        tmp3 = c.sb("lntmp3", [P, FC], F32)
        nfc = (t_len + FC - 1) // FC
        stats_done = 0
        apply_done = 0
        for f in range(nfc):
            fs = slice(f * FC, min((f + 1) * FC, t_len))
            fl = fs.stop - fs.start
            if f > 0:
                c.wait(nc.scalar, "t", stats_done)
            for k in range(nkd):
                i = nc.scalar.activation(sq[k][:, :fl], x_tiles[k][:, fs], AF.Square)
            c.inc(i, "s")
            if f > 0:
                c.wait(nc.tensor, "v", apply_done)
            c.wait(nc.tensor, "s")
            for k in range(nkd):
                nc.tensor.matmul(s1p[:, :fl], cst.ones_col[:], x_tiles[k][:, fs],
                                 start=(k == 0), stop=(k == nkd - 1))
                i = nc.tensor.matmul(s2p[:, :fl], cst.ones_col[:], sq[k][:, :fl],
                                     start=(k == 0), stop=(k == nkd - 1))
            stats_done = c.inc(i, "t")
            c.wait(nc.vector, "t", stats_done)
            c.sv(nc.vector.tensor_scalar_mul(mrow[:, :fl], s1p[:, :fl], 1.0 / dmodel))
            c.sv(nc.vector.tensor_mul(m2row[:, :fl], mrow[:, :fl], mrow[:, :fl]))
            c.sv(nc.vector.tensor_scalar_add(m2row[:, :fl], m2row[:, :fl], -float(eps)))
            i = nc.vector.scalar_tensor_tensor(vrow[:, :fl], s2p[:, :fl],
                                               1.0 / dmodel, m2row[:, :fl],
                                               ALU.mult, ALU.subtract)
            c.fin(i, "v", nc.vector)
            c.wait(nc.scalar, "v")
            c.ssc(nc.scalar.activation(vrow[:, :fl], vrow[:, :fl], AF.Ln))
            i = nc.scalar.activation(rrow[:, :fl], vrow[:, :fl], AF.Exp, scale=-0.5)
            c.fin(i, "s", nc.scalar)
            c.wait(nc.tensor, "s")
            nc.tensor.matmul(mb[:, :fl], cst.ones_row[:], mrow[:, :fl],
                             start=True, stop=True)
            i = nc.tensor.matmul(rb[:, :fl], cst.ones_row[:], rrow[:, :fl],
                                 start=True, stop=True)
            c.inc(i, "t")
            c.wait(nc.vector, "t")
            for k in range(nkd):
                c.sv(nc.vector.tensor_sub(tmp[:, :fl], x_tiles[k][:, fs], mb[:, :fl]))
                if gamma is not None:
                    c.sv(nc.vector.tensor_mul(tmp[:, :fl], tmp[:, :fl], rb[:, :fl]))
                    if out_bf is not None:
                        i = nc.vector.tensor_scalar(out_bf[k][:, fs], tmp[:, :fl],
                                                    gamma[k], beta[k],
                                                    ALU.mult, ALU.add)
                    if out_f32 is not None:
                        i = nc.vector.tensor_scalar(out_f32[k][:, fs],
                                                    tmp[:, :fl],
                                                    gamma[k], beta[k],
                                                    ALU.mult, ALU.add)
                elif out_bf is None:
                    i = nc.vector.tensor_mul(out_f32[k][:, fs], tmp[:, :fl],
                                             rb[:, :fl])
                else:
                    if out_lo is not None:
                        c.sv(nc.vector.tensor_mul(tmp3[:, :fl], tmp[:, :fl],
                                                  rb[:, :fl]))
                        c.sv(nc.vector.tensor_copy(out_bf[k][:, fs], tmp3[:, :fl]))
                        c.sv(nc.vector.tensor_copy(tmp2[:, :fl], out_bf[k][:, fs]))
                        c.sv(nc.vector.tensor_sub(tmp2[:, :fl], tmp3[:, :fl],
                                                  tmp2[:, :fl]))
                        i = nc.vector.tensor_copy(out_lo[k][:, fs], tmp2[:, :fl])
                    else:
                        i = nc.vector.tensor_mul(out_bf[k][:, fs], tmp[:, :fl],
                                                 rb[:, :fl])
                    if out_f32 is not None:
                        i = nc.vector.tensor_mul(out_f32[k][:, fs],
                                                 tmp[:, :fl], rb[:, :fl])
            apply_done = c.fin_val(i, "v", nc.vector)


def emit_ffn0(c, cst, dram, x_in, x_out, dbg=None):
    nc = c.nc
    c.phase_join()
    M = 1408
    NKM = M // P
    with c.scope():
        yacc = [c.sb("f0y", [P, T], F32) for _ in range(NKD)]
        xn = [c.sb("f0xn", [P, T], BF16) for _ in range(NKD)]
        gatT = c.sb("f0gat", [8, T], F32)
        xlo_scope = c.scope()
        xlo_es = xlo_scope.__enter__()
        xn_lo = [c.sb("f0xlo", [P, T], BF16) for _ in range(NKD)]
        scdump = c.sb("f0scd", [P, 8 * (T // P)], F32) if dbg and "scd" in dbg else None
        c.dma_join()
        if "st" in c.cnt:
            c.wait(nc.sync, "st")
        ld_x = load_tiles(c, yacc, x_in, NKD, T)
        for eng in (nc.tensor, nc.vector, nc.scalar):
            c.wait(eng, "ld", ld_x)
        emit_ln(c, cst, yacc, xn, T, out_lo=xn_lo)
        if dbg and "xlo0" in dbg:
            c.wait(nc.gpsimd, "v")
            i = nc.gpsimd.dma_start(out=dbg["xlo0"], in_=xn_lo[0][:])
            c.inc(i, "st", 16)
            i = nc.gpsimd.dma_start(out=dbg["xhi0"], in_=xn[0][:])
            c.inc(i, "st", 16)
        with c.scope():
            wgf = c.sb("f0wgf", [P, 8 * NKD], F32)
            wgb = c.sb("f0wgb", [P, 8 * NKD], BF16)
            wglo = c.sb("f0wglo", [P, 8 * NKD], BF16)
            c.dma_join()
            for k in range(NKD):
                i = nc.sync.dma_start(out=wgf[:, 8 * k:8 * k + 8],
                                      in_=dram["L0_ffn_w_gate"][k * P:(k + 1) * P, :])
                c.inc(i, "ld", 16)
            wgb_up = c.sb("f0wgbu", [P, 8 * NKD], F32)
            c.wait(nc.vector, "ld")
            c.sv(nc.vector.tensor_copy(wgb[:], wgf[:]))
            c.sv(nc.vector.tensor_copy(wgb_up[:], wgb[:]))
            c.sv(nc.vector.tensor_sub(wgb_up[:], wgf[:], wgb_up[:]))
            i = nc.vector.tensor_copy(wglo[:], wgb_up[:])
            c.fin(i, "v", nc.vector)
            c.wait(nc.tensor, "v")
            scp = c.ps("f0scp", [P, 8])
            gtp = c.ps("f0gtp", [8, P])
            sc_s = c.sb("f0sc", [P, 8], F32)
            d1 = c.sb("f0d1", [P, 8], F32)
            oh1 = c.sb("f0oh1", [P, 8], F32)
            msk = c.sb("f0msk", [P, 8], F32)
            em = c.sb("f0em", [P, 8], F32)
            kp2 = c.sb("f0kp2", [P, 8], F32)
            m1 = c.sb("f0m1", [P, 1], F32)
            nm1 = c.sb("f0nm1", [P, 1], F32)
            m2 = c.sb("f0m2", [P, 1], F32)
            nm2 = c.sb("f0nm2", [P, 1], F32)
            dq = c.sb("f0dq", [P, 1], F32)
            qq = c.sb("f0qq", [P, 1], F32)
            den = c.sb("f0den", [P, 1], F32)
            rec = c.sb("f0rec", [P, 1], F32)
            for tt in range(T // P):
                ts_ = slice(tt * P, (tt + 1) * P)
                for k in range(NKD):
                    nc.tensor.matmul(scp[:], xn[k][:, ts_], wgb[:, 8 * k:8 * k + 8],
                                     start=(k == 0), stop=False)
                    nc.tensor.matmul(scp[:], xn[k][:, ts_], wglo[:, 8 * k:8 * k + 8],
                                     start=False, stop=False)
                    i = nc.tensor.matmul(scp[:], xn_lo[k][:, ts_],
                                         wgb[:, 8 * k:8 * k + 8],
                                         start=False, stop=(k == NKD - 1))
                c.inc(i, "t")
                c.wait(nc.vector, "t")
                v = nc.vector
                c.sv(v.tensor_copy(sc_s[:], scp[:]))
                if scdump is not None:
                    c.sv(v.tensor_copy(scdump[:, 8 * tt:8 * tt + 8], sc_s[:]))
                c.sv(v.reduce_max(m1[:, 0:1], sc_s[:], axis=AX.X))
                c.sv(v.tensor_scalar_mul(nm1[:, 0:1], m1[:, 0:1], -1.0))
                c.sv(v.tensor_scalar_add(d1[:], sc_s[:], nm1[:, 0:1]))
                c.sv(v.tensor_single_scalar(oh1[:], d1[:], 0.0, ALU.is_ge))
                c.sv(v.scalar_tensor_tensor(msk[:], oh1[:], -1e30, sc_s[:],
                                            ALU.mult, ALU.add))
                c.sv(v.reduce_max(m2[:, 0:1], msk[:], axis=AX.X))
                c.sv(v.tensor_scalar_mul(nm2[:, 0:1], m2[:, 0:1], -1.0))
                i = v.tensor_sub(dq[:, 0:1], nm1[:, 0:1], nm2[:, 0:1])
                c.fin(i, "v", nc.vector)
                c.wait(nc.scalar, "v")
                c.ssc(nc.scalar.activation(em[:], d1[:], AF.Exp))
                i = nc.scalar.activation(qq[:, 0:1], dq[:, 0:1], AF.Exp)
                c.fin(i, "s", nc.scalar)
                c.wait(nc.vector, "s")
                c.sv(v.tensor_scalar_add(den[:, 0:1], qq[:, 0:1], 1.0))
                c.sv(v.reciprocal(rec[:, 0:1], den[:, 0:1]))
                c.sv(v.tensor_scalar_add(d1[:], sc_s[:], nm2[:, 0:1]))
                c.sv(v.tensor_single_scalar(kp2[:], d1[:], 0.0, ALU.is_ge))
                c.sv(v.tensor_mul(em[:], em[:], kp2[:]))
                i = v.tensor_scalar_mul(em[:], em[:], rec[:, 0:1])
                c.fin(i, "v", nc.vector)
                c.wait(nc.tensor, "v")
                i = nc.tensor.transpose(gtp[:], em[:], cst.ident_f[:])
                c.inc(i, "t")
                c.wait(nc.vector, "t")
                i = v.tensor_copy(gatT[:, ts_], gtp[:])
                c.inc(i, "v")
                c.wait(nc.tensor, "v")
        if dbg and "gatT" in dbg:
            c.wait(nc.gpsimd, "v")
            i = nc.gpsimd.dma_start(out=dbg["gatT"], in_=gatT[:])
            c.inc(i, "st", 16)
            if "scd" in dbg:
                i = nc.gpsimd.dma_start(out=dbg["scd"], in_=scdump[:])
                c.inc(i, "st", 16)
        xlo_scope.__exit__(None, None, None)
        with c.scope():
            wl = [c.sb("f0wl", [P, M], F32) for _ in range(2)]
            wg_bf = [c.sb("f0wgbf", [P, M], BF16) for _ in range(NKD)]
            wu_bf = [c.sb("f0wubf", [P, M], BF16) for _ in range(NKD)]
            wd_bf = [c.sb("f0wdbf", [P, D], BF16) for _ in range(NKM)]
            hg = c.sb("f0hg", [P, FC], BF16)
            hid2 = [c.sb("f0hid", [P, FC], BF16) for _ in range(NKM)]
            pg = c.ps("f0pg", [P, FC])
            pu = c.ps("f0pu", [P, FC])
            pgb = c.ps("f0pgb", [P, FC])
            py = [c.ps("f0py", [P, FC]) for _ in range(2)]
            py_free = [0, 0]
            c.dma_join()
            cast_done = []
            for e in range(8):

                def stream(dst_list, src3, ntile, width):
                    for k in range(ntile):
                        if len(cast_done) >= 2:
                            c.wait(nc.sync, "v", cast_done[-2])
                        i = nc.sync.dma_start(out=wl[len(cast_done) % 2][:, :width],
                                              in_=src3[k * P:(k + 1) * P, :width])
                        c.inc(i, "ld", 16)
                        c.wait(nc.vector, "ld")
                        i = nc.vector.tensor_copy(dst_list[k][:, :width],
                                                  wl[len(cast_done) % 2][:, :width])
                        cast_done.append(c.inc(i, "v"))
                stream(wg_bf, dram["L0_ffn_w_g"][e], NKD, M)
                stream(wu_bf, dram["L0_ffn_w_u"][e], NKD, M)
                stream(wd_bf, dram["L0_ffn_w_d"][e], NKM, D)
                c.wait(nc.tensor, "v")
                for f in range(NFC):
                    fs = slice(f * FC, (f + 1) * FC)
                    i = nc.tensor.matmul(pgb[:], cst.sel8[:, e * P:(e + 1) * P],
                                         gatT[:, fs], start=True, stop=True)
                    for m in range(NKM):
                        ms = slice(m * P, (m + 1) * P)
                        for k in range(NKD):
                            nc.tensor.matmul(pg[:], wg_bf[k][:, ms], xn[k][:, fs],
                                             start=(k == 0), stop=(k == NKD - 1))
                        for k in range(NKD):
                            i = nc.tensor.matmul(pu[:], wu_bf[k][:, ms], xn[k][:, fs],
                                                 start=(k == 0), stop=(k == NKD - 1))
                        c.inc(i, "t")
                        c.wait(nc.scalar, "t")
                        i = nc.scalar.activation(hg[:], pg[:], AF.Silu)
                        c.inc(i, "s")
                        c.wait(nc.vector, "s")
                        c.sv(nc.vector.tensor_tensor(hid2[m][:], hg[:], pgb[:], ALU.mult))
                        i = nc.vector.tensor_tensor(hid2[m][:], hid2[m][:], pu[:],
                                                    ALU.mult)
                        c.fin(i, "v", nc.vector)
                        c.wait(nc.tensor, "v")
                    if dbg and "hid0" in dbg and e == 0 and f == 0:
                        c.wait(nc.gpsimd, "v")
                        for m in range(NKM):
                            i = nc.gpsimd.dma_start(
                                out=dbg["hid0"][m * P:(m + 1) * P, :], in_=hid2[m][:])
                            c.inc(i, "st", 16)
                        c.wait(nc.tensor, "st")
                    for mo in range(NKD):
                        mos = slice(mo * P, (mo + 1) * P)
                        pyt = py[mo % 2]
                        c.wait(nc.tensor, "v", py_free[mo % 2])
                        for k in range(NKM):
                            i = nc.tensor.matmul(pyt[:], wd_bf[k][:, mos], hid2[k][:],
                                                 start=(k == 0), stop=(k == NKM - 1))
                        c.inc(i, "t")
                        c.wait(nc.vector, "t")
                        i = nc.vector.scalar_tensor_tensor(
                            yacc[mo][:, fs], pyt[:], 0.5, yacc[mo][:, fs],
                            ALU.mult, ALU.add)
                        py_free[mo % 2] = c.inc(i, "v")
                    c.wait(nc.tensor, "v")
                if dbg and "y0" in dbg and e == 0:
                    c.wait(nc.gpsimd, "v")
                    for k in range(NKD):
                        i = nc.gpsimd.dma_start(out=dbg["y0"][k * P:(k + 1) * P, :],
                                                in_=yacc[k][:])
                        c.inc(i, "st", 16)
                    c.wait(nc.tensor, "st")
        c.wait(nc.gpsimd, "v")
        for k in range(NKD):
            i = nc.gpsimd.dma_start(out=x_out[k * P:(k + 1) * P, :], in_=yacc[k][:])
            c.inc(i, "st", 16)
    return c.cnt["st"]


def finish(c):
    c.wait(c.nc.gpsimd, "st")


def build_program(phase_fns, input_specs, output_specs, consts_need=None,
                  scratch_specs=None):
    nc = bass.Bass("TRN2", target_bir_lowering=False)
    dram = {}
    for name, (shape, dt) in input_specs.items():
        dram[name] = nc.dram_tensor(name, list(shape), dt, kind="ExternalInput").ap()
    for name, (shape, dt) in output_specs.items():
        dram[name] = nc.dram_tensor(name, list(shape), dt, kind="ExternalOutput").ap()
    for name, (shape, dt) in (scratch_specs or {}).items():
        dram[name] = nc.dram_tensor(name, list(shape), dt, kind="Internal").ap()
    c = C(nc)
    block = c.es.enter_context(nc.Block())

    @block.sync
    def _(sync):
        cst = (Consts(c, dram) if consts_need is None
               else Consts(c, dram, need=consts_need))
        cst.wait_all(c)
        for fn in phase_fns:
            fn(c, cst, dram)
        finish(c)
    c.close()
    return nc


def emit_mamba(c, cst, dram, x_in, x_out, dbg=None):
    nc = c.nc
    c.phase_join()
    DI = 2048
    NDT = DI // P
    TC = 256
    NCH = T // TC
    with c.scope():
        lng = c.sb("mb_lng", [P, NKD], F32)
        lnb = c.sb("mb_lnb", [P, NKD], F32)
        cw = c.sb("mb_cw", [P, 4 * NDT], F32)
        cb = c.sb("mb_cb", [P, NDT], F32)
        bd = c.sb("mb_bd", [P, NDT], F32)
        dp = c.sb("mb_dp", [P, NDT], F32)
        logA = c.sb("mb_logA", [P, 16 * NDT], F32)
        A_sb = c.sb("mb_A", [P, 16 * NDT], F32)
        wb_bf = [c.sb("mb_wb", [P, 16], BF16) for _ in range(NDT)]
        wc_bf = [c.sb("mb_wc", [P, 16], BF16) for _ in range(NDT)]
        wsm = c.sb("mb_wsm", [P, 16 * 2], F32)
        carry_h = [c.sb("mb_ch", [P, 16], F32) for _ in range(NDT)]
        carry3 = [c.sb("mb_c3", [P, 4], F32) for _ in range(NDT)]
        c.dma_join()
        for k in range(NKD):
            i = nc.sync.dma_start(out=lng[:, k:k + 1], in_=dram["L0_attn_ln_g"][k])
            c.inc(i, "ld", 16)
            i = nc.sync.dma_start(out=lnb[:, k:k + 1], in_=dram["L0_attn_ln_b"][k])
            c.inc(i, "ld", 16)
        for dt in range(NDT):
            i = nc.sync.dma_start(out=cw[:, 4 * dt:4 * dt + 4],
                                  in_=dram["L0_attn_conv_w"][dt])
            c.inc(i, "ld", 16)
            i = nc.sync.dma_start(out=cb[:, dt:dt + 1], in_=dram["L0_attn_conv_b"][dt])
            c.inc(i, "ld", 16)
            i = nc.sync.dma_start(out=bd[:, dt:dt + 1], in_=dram["L0_attn_b_delta"][dt])
            c.inc(i, "ld", 16)
            i = nc.sync.dma_start(out=dp[:, dt:dt + 1], in_=dram["L0_attn_Dp"][dt])
            c.inc(i, "ld", 16)
            i = nc.sync.dma_start(out=logA[:, 16 * dt:16 * dt + 16],
                                  in_=dram["L0_attn_log_A"][dt])
            c.inc(i, "ld", 16)
            i = nc.sync.dma_start(out=wsm[:, 0:16],
                                  in_=dram["L0_attn_W_b"][dt * P:(dt + 1) * P, :])
            c.inc(i, "ld", 16)
            c.wait(nc.vector, "ld")
            c.sv(nc.vector.tensor_copy(wb_bf[dt][:], wsm[:, 0:16]))
            i = nc.sync.dma_start(out=wsm[:, 16:32],
                                  in_=dram["L0_attn_W_c"][dt * P:(dt + 1) * P, :])
            c.inc(i, "ld", 16)
            c.wait(nc.vector, "ld")
            i = nc.vector.tensor_copy(wc_bf[dt][:], wsm[:, 16:32])
            c.fin(i, "v", nc.vector)
            c.dma_join(names=("v",))
        c.wait(nc.scalar, "ld")
        i = nc.scalar.activation(A_sb[:], logA[:], AF.Exp)
        c.fin(i, "s", nc.scalar)
        c.wait(nc.vector, "s")
        c.sv(nc.vector.tensor_scalar_mul(A_sb[:], A_sb[:], -1.0))
        for dt in range(NDT):
            c.sv(nc.vector.memset(carry_h[dt][:], 0.0))
            i = nc.vector.memset(carry3[dt][:], 0.0)
        c.fin(i, "v", nc.vector)
        x_sb = [c.sb("mb_x", [P, TC], F32) for _ in range(NKD)]
        u_sb = [c.sb("mb_u0", [P, TC], F32) for _ in range(NKD)]
        xnb = [c.sb("mb_xn", [P, TC], BF16) for _ in range(NKD)]
        xb_f = [c.sb("mb_xbf", [P, TC], F32) for _ in range(NDT)]
        xb_b = [c.sb("mb_xbb", [P, TC], BF16) for _ in range(NDT)]
        sz_b = [c.sb("mb_sz", [P, TC], BF16) for _ in range(NDT)]
        delta = [c.sb("mb_dl", [P, TC], F32) for _ in range(NDT)]
        dxb = [c.sb("mb_dxb", [P, TC], F32) for _ in range(NDT)]
        yT = [c.sb("mb_yT", [P, TC], F32) for _ in range(NDT)]
        ygb = xb_b
        BT = c.sb("mb_BT", [16, TC], F32)
        CT = c.sb("mb_CT", [16, TC], F32)
        xbr = c.sb("mb_xbr", [P, TC + 4], F32)
        xc_t = c.sb("mb_xc", [P, TC], F32)
        a_buf2 = [c.sb("mb_a", [P, TC], F32) for _ in range(2)]
        a_buf = a_buf2[0]
        u_buf = c.sb("mb_u", [P, TC], F32)
        h_buf = c.sb("mb_h", [P, TC], F32)
        sp1 = a_buf
        sp2 = u_buf
        for ch in range(NCH):
            cs = slice(ch * TC, (ch + 1) * TC)
            c.dma_join()
            if "st" in c.cnt:
                c.wait(nc.sync, "st")
            for k in range(NKD):
                i = nc.sync.dma_start(out=x_sb[k][:],
                                      in_=x_in[k * P:(k + 1) * P, cs])
                c.inc(i, "ld", 16)
            ldx = c.cnt["ld"]
            for eng in (nc.tensor, nc.vector, nc.scalar):
                c.wait(eng, "ld", ldx)
            emit_ln(c, cst, x_sb, None, TC, out_f32=u_sb)
            c.wait(nc.scalar, "v")
            c.wait(nc.tensor, "v")
            emit_ln(c, cst, u_sb, xnb, TC,
                    gamma=[lng[:, k:k + 1] for k in range(NKD)],
                    beta=[lnb[:, k:k + 1] for k in range(NKD)])
            for half in range(2):
                with c.scope():
                    wl = [c.sb("mb_wl", [P, 2048], F32) for _ in range(2)]
                    win_bf = [c.sb("mb_win", [P, 2048], BF16) for _ in range(NKD)]
                    cd = []
                    c.dma_join()
                    for k in range(NKD):
                        if len(cd) >= 2:
                            c.wait(nc.sync, "v", cd[-2])
                        i = nc.sync.dma_start(
                            out=wl[k % 2][:],
                            in_=dram["L0_attn_W_in"][k * P:(k + 1) * P,
                                                     half * 2048:(half + 1) * 2048])
                        c.inc(i, "ld", 16)
                        c.wait(nc.vector, "ld")
                        i = nc.vector.tensor_copy(win_bf[k][:], wl[k % 2][:])
                        cd.append(c.inc(i, "v"))
                    c.wait(nc.tensor, "v")
                    pz = c.ps("mb_pz", [P, TC])
                    for m in range(16):
                        dt = m
                        ms = slice(m * P, (m + 1) * P)
                        for k in range(NKD):
                            i = nc.tensor.matmul(pz[:], win_bf[k][:, ms],
                                                 xnb[k][:], start=(k == 0),
                                                 stop=(k == NKD - 1))
                        c.inc(i, "t")
                        if half == 1:
                            c.wait(nc.scalar, "t")
                            i = nc.scalar.activation(sz_b[dt][:], pz[:], AF.Silu)
                            c.fin(i, "s", nc.scalar)
                            c.wait(nc.tensor, "s")
                        else:
                            c.wait(nc.vector, "t")
                            c.sv(nc.vector.tensor_copy(xbr[:, 0:3],
                                                       carry3[dt][:, 0:3]))
                            c.sv(nc.vector.tensor_copy(xbr[:, 3:TC + 3], pz[:]))
                            c.sv(nc.vector.tensor_copy(carry3[dt][:, 0:3],
                                                       xbr[:, TC:TC + 3]))
                            c.sv(nc.vector.tensor_scalar_mul(
                                xc_t[:], xbr[:, 0:TC], cw[:, 4 * dt:4 * dt + 1]))
                            for j in range(1, 4):
                                c.sv(nc.vector.scalar_tensor_tensor(
                                    xc_t[:], xbr[:, j:j + TC],
                                    cw[:, 4 * dt + j:4 * dt + j + 1], xc_t[:],
                                    ALU.mult, ALU.add))
                            i = nc.vector.engine_nop()
                            c.fin(i, "v", nc.vector)
                            c.wait(nc.scalar, "v")
                            i = nc.scalar.activation(xb_f[dt][:], xc_t[:], AF.Silu,
                                                     bias=cb[:, dt:dt + 1])
                            c.fin(i, "s", nc.scalar)
                            c.wait(nc.vector, "s")
                            i = nc.vector.tensor_copy(xb_b[dt][:], xb_f[dt][:])
                            c.fin(i, "v", nc.vector)
                            c.wait(nc.tensor, "v")
            if dbg and ch == 0:
                c.wait(nc.gpsimd, "v")
                c.wait(nc.gpsimd, "s")
                for k in range(NKD):
                    i = nc.gpsimd.dma_start(out=dbg["u0"][k * P:(k + 1) * P, :],
                                            in_=u_sb[k][:])
                    c.inc(i, "st", 16)
                for dt in range(NDT):
                    i = nc.gpsimd.dma_start(out=dbg["xb0"][dt * P:(dt + 1) * P, :],
                                            in_=xb_f[dt][:])
                    c.inc(i, "st", 16)
                    i = nc.gpsimd.dma_start(out=dbg["sz0"][dt * P:(dt + 1) * P, :],
                                            in_=sz_b[dt][:])
                    c.inc(i, "st", 16)
                c.wait(nc.tensor, "st")
            with c.scope():
                pbc = c.ps("mb_pbc", [16, TC])
                for dt in range(NDT):
                    i = nc.tensor.matmul(pbc[:], wb_bf[dt][:], xb_b[dt][:],
                                         start=(dt == 0), stop=(dt == NDT - 1))
                c.inc(i, "t")
                c.wait(nc.vector, "t")
                i = nc.vector.tensor_copy(BT[:], pbc[:])
                c.fin(i, "v", nc.vector)
                c.wait(nc.tensor, "v")
                for dt in range(NDT):
                    i = nc.tensor.matmul(pbc[:], wc_bf[dt][:], xb_b[dt][:],
                                         start=(dt == 0), stop=(dt == NDT - 1))
                c.inc(i, "t")
                c.wait(nc.vector, "t")
                i = nc.vector.tensor_copy(CT[:], pbc[:])
                c.fin(i, "v", nc.vector)
            for half in range(2):
                with c.scope():
                    wl = [c.sb("mb_wdl", [P, 1024], F32) for _ in range(2)]
                    wd_bf = [c.sb("mb_wd", [P, 1024], BF16) for _ in range(NDT)]
                    cd = []
                    c.dma_join()
                    for k in range(NDT):
                        if len(cd) >= 2:
                            c.wait(nc.sync, "v", cd[-2])
                        i = nc.sync.dma_start(
                            out=wl[k % 2][:],
                            in_=dram["L0_attn_W_delta"][k * P:(k + 1) * P,
                                                        half * 1024:(half + 1) * 1024])
                        c.inc(i, "ld", 16)
                        c.wait(nc.vector, "ld")
                        i = nc.vector.tensor_copy(wd_bf[k][:], wl[k % 2][:])
                        cd.append(c.inc(i, "v"))
                    c.wait(nc.tensor, "v")
                    pd = c.ps("mb_pd", [P, TC])
                    for m in range(8):
                        dt = half * 8 + m
                        ms = slice(m * P, (m + 1) * P)
                        for k in range(NDT):
                            i = nc.tensor.matmul(pd[:], wd_bf[k][:, ms],
                                                 xb_b[k][:], start=(k == 0),
                                                 stop=(k == NDT - 1))
                        c.inc(i, "t")
                        c.wait(nc.scalar, "t")
                        c.ssc(nc.scalar.activation(sp1[:], pd[:], AF.Relu,
                                                   bias=bd[:, dt:dt + 1]))
                        c.ssc(nc.scalar.activation(sp2[:], pd[:], AF.Abs,
                                                   bias=bd[:, dt:dt + 1]))
                        c.ssc(nc.scalar.activation(sp2[:], sp2[:], AF.Exp,
                                                   scale=-1.0))
                        i = nc.scalar.activation(sp2[:], sp2[:], AF.Ln,
                                                 bias=cst.kone[:, 0:1])
                        c.fin(i, "s", nc.scalar)
                        c.wait(nc.vector, "s")
                        c.sv(nc.vector.tensor_add(delta[dt][:], sp1[:], sp2[:]))
                        i = nc.vector.tensor_mul(dxb[dt][:], delta[dt][:],
                                                 xb_f[dt][:])
                        c.fin(i, "v", nc.vector)
                        c.wait(nc.tensor, "v")
                        c.wait(nc.scalar, "v")
            if dbg and ch == 0:
                c.wait(nc.gpsimd, "v")
                i = nc.gpsimd.dma_start(out=dbg["BT0"], in_=BT[:])
                c.inc(i, "st", 16)
                i = nc.gpsimd.dma_start(out=dbg["CT0"], in_=CT[:])
                c.inc(i, "st", 16)
                for dt in range(NDT):
                    i = nc.gpsimd.dma_start(out=dbg["dl0"][dt * P:(dt + 1) * P, :],
                                            in_=delta[dt][:])
                    c.inc(i, "st", 16)
                c.wait(nc.tensor, "st")
            with c.scope():
                pB = c.ps("mb_pB", [P, TC])
                pC = c.ps("mb_pC", [P, TC])
                vdone = []
                adone = []
                pairs = [(s, dt) for s in range(16) for dt in range(NDT)]
                for s in range(16):
                    nc.tensor.matmul(pB[:], cst.sel16[:, s * P:(s + 1) * P], BT[:],
                                     start=True, stop=True)
                    i = nc.tensor.matmul(pC[:], cst.sel16[:, s * P:(s + 1) * P],
                                         CT[:], start=True, stop=True)
                    c.inc(i, "t")
                    c.wait(nc.vector, "t")
                    c.wait(nc.scalar, "t")
                    for dt in range(NDT):
                        j = len(vdone)
                        if j >= 2:
                            c.wait(nc.scalar, "v", vdone[j - 2])
                        i = nc.scalar.activation(a_buf2[j % 2][:], delta[dt][:],
                                                 AF.Exp,
                                                 scale=A_sb[:, 16 * dt + s:
                                                            16 * dt + s + 1])
                        adone.append(c.fin_val(i, "s", nc.scalar))
                        c.sv(nc.vector.tensor_tensor(u_buf[:], dxb[dt][:],
                                                     pB[:], ALU.mult))
                        c.wait(nc.vector, "s", adone[j])
                        c.sv(nc.vector.tensor_tensor_scan(
                            h_buf[:], a_buf2[j % 2][:], u_buf[:],
                            carry_h[dt][:, s:s + 1], ALU.mult, ALU.add))
                        c.sv(nc.vector.tensor_copy(carry_h[dt][:, s:s + 1],
                                                   h_buf[:, TC - 1:TC]))
                        if s == 0:
                            i = nc.vector.tensor_tensor(yT[dt][:], h_buf[:],
                                                        pC[:], ALU.mult)
                        else:
                            c.sv(nc.vector.tensor_tensor(u_buf[:], h_buf[:],
                                                         pC[:], ALU.mult))
                            i = nc.vector.tensor_add(yT[dt][:], yT[dt][:],
                                                     u_buf[:])
                        vdone.append(c.fin_val(i, "v", nc.vector))
                    c.wait(nc.tensor, "v")
            if dbg and ch == 0:
                c.wait(nc.gpsimd, "v")
                for dt in range(NDT):
                    i = nc.gpsimd.dma_start(out=dbg["yT0"][dt * P:(dt + 1) * P, :],
                                            in_=yT[dt][:])
                    c.inc(i, "st", 16)
                c.wait(nc.tensor, "st")
                c.wait(nc.vector, "st")
            for dt in range(NDT):
                c.sv(nc.vector.scalar_tensor_tensor(
                    yT[dt][:], xb_f[dt][:], dp[:, dt:dt + 1], yT[dt][:],
                    ALU.mult, ALU.add))
                c.sv(nc.vector.tensor_copy(a_buf[:], sz_b[dt][:]))
                i = nc.vector.tensor_tensor(ygb[dt][:], yT[dt][:], a_buf[:],
                                            ALU.mult)
                c.fin(i, "v", nc.vector)
            if dbg and ch == 0:
                c.wait(nc.gpsimd, "v")
                for dt in range(NDT):
                    i = nc.gpsimd.dma_start(out=dbg["yg0"][dt * P:(dt + 1) * P, :],
                                            in_=ygb[dt][:])
                    c.inc(i, "st", 16)
                c.wait(nc.tensor, "st")
                c.wait(nc.vector, "st")
            with c.scope():
                wl = [c.sb("mb_wol", [P, 1024], F32) for _ in range(2)]
                wo_bf = [c.sb("mb_wo", [P, 1024], BF16) for _ in range(NDT)]
                cd = []
                c.dma_join()
                for k in range(NDT):
                    if len(cd) >= 2:
                        c.wait(nc.sync, "v", cd[-2])
                    i = nc.sync.dma_start(
                        out=wl[k % 2][:],
                        in_=dram["L0_attn_W_out"][k * P:(k + 1) * P, :])
                    c.inc(i, "ld", 16)
                    c.wait(nc.vector, "ld")
                    i = nc.vector.tensor_copy(wo_bf[k][:], wl[k % 2][:])
                    cd.append(c.inc(i, "v"))
                c.wait(nc.tensor, "v")
                po = c.ps("mb_po", [P, TC])
                for mo in range(NKD):
                    mos = slice(mo * P, (mo + 1) * P)
                    for k in range(NDT):
                        i = nc.tensor.matmul(po[:], wo_bf[k][:, mos], ygb[k][:],
                                             start=(k == 0), stop=(k == NDT - 1))
                    c.inc(i, "t")
                    c.wait(nc.vector, "t")
                    c.sv(nc.vector.tensor_add(x_sb[mo][:], x_sb[mo][:], po[:]))
                    i = nc.vector.tensor_add(x_sb[mo][:], x_sb[mo][:],
                                             u_sb[mo][:])
                    c.fin(i, "v", nc.vector)
                    c.wait(nc.tensor, "v")
                c.wait(nc.gpsimd, "v")
                for k in range(NKD):
                    i = nc.gpsimd.dma_start(out=x_out[k * P:(k + 1) * P, cs],
                                            in_=x_sb[k][:])
                    c.inc(i, "st", 16)


def emit_l1router(c, cst, dram, x_in, hid_out, emb_out, oh_out, rp_out, cnt_out):
    nc = c.nc
    c.phase_join()
    H, DH = 16, 64
    with c.scope():
        embed_bf = [c.sb("r_emb", [P, T], BF16) for _ in range(NKD)]
        with c.scope():
            x_sb = [c.sb("r_x", [P, T], F32) for _ in range(NKD)]
            embf = [c.sb("r_ef", [P, T], F32) for _ in range(NKD)]
            c.dma_join()
            if "st" in c.cnt:
                c.wait(nc.sync, "st")
            ldx = load_tiles(c, x_sb, x_in, NKD, T)
            for eng in (nc.tensor, nc.vector, nc.scalar):
                c.wait(eng, "ld", ldx)
            emit_ln(c, cst, x_sb, embed_bf, T, out_f32=embf)
            c.wait(nc.gpsimd, "v")
            for k in range(NKD):
                i = nc.gpsimd.dma_start(out=emb_out[k * P:(k + 1) * P, :],
                                        in_=embf[k][:])
                c.inc(i, "st", 16)
        hidT = [c.sb("r_hid", [P, T], BF16) for _ in range(NKD)]
        bq = c.sb("r_bq", [DH, H], F32)
        bq32 = c.sb("r_bq32", [DH, H], F32)
        bk = c.sb("r_bk", [DH, H], F32)
        bvr = c.sb("r_bvr", [1, DH * H], F32)
        c.dma_join(names=("v",))
        c.wait(nc.sync, "st")
        for h in range(H):
            i = nc.sync.dma_start(out=bq[:, h:h + 1], in_=dram["L1_ffn_bq"][h])
            c.inc(i, "ld", 16)
            i = nc.sync.dma_start(out=bk[:, h:h + 1], in_=dram["L1_ffn_bk"][h])
            c.inc(i, "ld", 16)
            i = nc.sync.dma_start(out=bvr[:, h * DH:(h + 1) * DH],
                                  in_=dram["L1_ffn_bv"][h])
            c.inc(i, "ld", 16)
        c.wait(nc.vector, "ld")
        i = nc.vector.tensor_scalar_mul(bq32[:], bq[:], 1.0 / 32.0)
        c.fin(i, "v", nc.vector)
        with c.scope():
            qT = c.sb("r_qT", [DH, T], BF16)
            kT = c.sb("r_kT", [DH, T], BF16)
            V = [c.sb("r_V", [P, DH], BF16) for _ in range(T // P)]
            p_sb = c.sb("r_p", [P, T], BF16)
            pT_sb = c.sb("r_pT", [P, P], BF16)
            bvb = c.sb("r_bvb", [P, DH], F32)
            vcs = c.sb("r_vcs", [1, DH * 18], BF16)
            vcf = c.sb("r_vcf", [1, DH * 17], F32)
            w0 = c.sb("r_w0", [P, 1], F32)
            w0r = c.sb("r_w0r", [1, P], BF16)
            m0a = c.sb("r_m0a", [P, 1], F32)
            m0b = c.sb("r_m0b", [P, 1], F32)
            negm = c.sb("r_negm", [P, 1], F32)
            enm = c.sb("r_enm", [P, 1], F32)
            za = c.sb("r_za", [P, 1], F32)
            zb = c.sb("r_zb", [P, 1], F32)
            rcp = c.sb("r_rcp", [P, 1], F32)
            tmpd = c.sb("r_tmpd", [P, P], F32)
            wl = [c.sb("r_wl", [P, DH], F32) for _ in range(2)]
            wq_bf = [c.sb("r_wqb", [P, DH], BF16) for _ in range(NKD)]
            wk_bf = [c.sb("r_wkb", [P, DH], BF16) for _ in range(NKD)]
            wv_bf = [c.sb("r_wvb", [P, DH], BF16) for _ in range(NKD)]
            cd = []

            def stream(dst_list, src3):
                for k in range(NKD):
                    if len(cd) >= 2:
                        c.wait(nc.sync, "v", cd[-2])
                    i = nc.sync.dma_start(out=wl[len(cd) % 2][:],
                                          in_=src3[k * P:(k + 1) * P, :])
                    c.inc(i, "ld", 16)
                    c.wait(nc.vector, "ld")
                    i = nc.vector.tensor_copy(dst_list[k][:],
                                              wl[len(cd) % 2][:])
                    cd.append(c.inc(i, "v"))
            c.dma_join()
            for h in range(H):
              with c.scope():
                pq = c.ps("r_pq", [DH, FC])
                pv = c.ps("r_pv", [P, DH])
                pvc = c.ps("r_pvc", [1, DH])
                stream(wq_bf, dram["L1_ffn_Wq"][h])
                stream(wk_bf, dram["L1_ffn_Wk"][h])
                stream(wv_bf, dram["L1_ffn_Wv"][h])
                c.wait(nc.tensor, "v")
                for f in range(NFC):
                    fs = slice(f * FC, (f + 1) * FC)
                    for k in range(NKD):
                        i = nc.tensor.matmul(pq[:], wq_bf[k][:], embed_bf[k][:, fs],
                                             start=(k == 0), stop=(k == NKD - 1))
                    c.inc(i, "t")
                    c.wait(nc.scalar, "t")
                    i = nc.scalar.activation(qT[:, fs], pq[:], AF.Identity,
                                             bias=bq32[:, h:h + 1], scale=1.0 / 32)
                    c.fin(i, "s", nc.scalar)
                    c.wait(nc.tensor, "s")
                    for k in range(NKD):
                        i = nc.tensor.matmul(pq[:], wk_bf[k][:], embed_bf[k][:, fs],
                                             start=(k == 0), stop=(k == NKD - 1))
                    c.inc(i, "t")
                    c.wait(nc.scalar, "t")
                    i = nc.scalar.activation(kT[:, fs], pq[:], AF.Identity,
                                             bias=bk[:, h:h + 1])
                    c.fin(i, "s", nc.scalar)
                    c.wait(nc.tensor, "s")
                i = nc.tensor.matmul(pv[:], cst.ones_row[:],
                                     bvr[0:1, h * DH:(h + 1) * DH],
                                     start=True, stop=True)
                c.inc(i, "t")
                c.wait(nc.vector, "t")
                i = nc.vector.tensor_copy(bvb[:], pv[:])
                c.fin(i, "v", nc.vector)
                c.wait(nc.tensor, "v")
                for tc in range(T // P):
                    tcs = slice(tc * P, (tc + 1) * P)
                    for k in range(NKD):
                        i = nc.tensor.matmul(pv[:], embed_bf[k][:, tcs], wv_bf[k][:],
                                             start=(k == 0), stop=(k == NKD - 1))
                    c.inc(i, "t")
                    c.wait(nc.vector, "t")
                    i = nc.vector.tensor_tensor(V[tc][:], pv[:], bvb[:], ALU.add)
                    c.fin(i, "v", nc.vector)
                    c.wait(nc.tensor, "v")
                    i = nc.tensor.matmul(pvc[:], cst.ones_col_bf[:], V[tc][:],
                                         start=True, stop=True)
                    c.inc(i, "t")
                    c.wait(nc.vector, "t")
                    i = nc.vector.tensor_copy(vcf[0:1, tc * DH:(tc + 1) * DH],
                                              pvc[:])
                    c.fin(i, "v", nc.vector)
                    c.wait(nc.tensor, "v")
                c.sv(nc.vector.memset(vcs[0:1, 16 * DH:17 * DH], 0.0))
                c.sv(nc.vector.memset(vcf[0:1, 16 * DH:17 * DH], 0.0))
                for tc in range(15, -1, -1):
                    c.sv(nc.vector.tensor_add(
                        vcf[0:1, tc * DH:(tc + 1) * DH],
                        vcf[0:1, tc * DH:(tc + 1) * DH],
                        vcf[0:1, (tc + 1) * DH:(tc + 2) * DH]))
                    c.sv(nc.vector.tensor_copy(vcs[0:1, tc * DH:(tc + 1) * DH],
                                               vcf[0:1, tc * DH:(tc + 1) * DH]))
                i = nc.vector.engine_nop()
                c.fin(i, "v", nc.vector)
              with c.scope():
                    natt = c.ps("r_att", [P, T])
                    ptp = c.ps("r_ptp", [P, P], BF16)
                    pot = c.ps("r_pot", [DH, P])
                    pw0 = c.ps("r_pw0", [1, P])
                    for qc in range(T // P):
                        L = P * (qc + 1)
                        qs = slice(qc * P, (qc + 1) * P)
                        nblk = (L + FC - 1) // FC
                        for b_ in range(nblk):
                            F = min(FC, L - b_ * FC)
                            i = nc.tensor.matmul(
                                natt[:, b_ * FC:b_ * FC + F], qT[:, qs],
                                kT[:, b_ * FC:b_ * FC + F],
                                start=True, stop=True)
                        c.inc(i, "t")
                        c.wait(nc.vector, "t")
                        c.sv(nc.vector.tensor_tensor(tmpd[:], natt[:, qs],
                                                     cst.tril_f[:], ALU.mult))
                        if qc > 0:
                            c.sv(nc.vector.reduce_max(m0a[:, 0:1],
                                                      natt[:, 0:qc * P], axis=AX.X))
                        c.sv(nc.vector.reduce_max(m0b[:, 0:1], tmpd[:], axis=AX.X))
                        if qc > 0:
                            c.sv(nc.vector.tensor_max(m0b[:, 0:1], m0b[:, 0:1],
                                                      m0a[:, 0:1]))
                        c.sv(nc.vector.tensor_scalar_max(m0b[:, 0:1], m0b[:, 0:1],
                                                         0.0))
                        i = nc.vector.tensor_scalar_mul(negm[:, 0:1], m0b[:, 0:1],
                                                        -1.0)
                        c.fin(i, "v", nc.vector)
                        c.wait(nc.scalar, "v")
                        if qc > 0:
                            c.ssc(nc.scalar.activation(p_sb[:, 0:qc * P],
                                                       natt[:, 0:qc * P], AF.Exp,
                                                       bias=negm[:, 0:1],
                                                       accum_out=za[:, 0:1]))
                        c.ssc(nc.scalar.activation(p_sb[:, qs], tmpd[:], AF.Exp,
                                                   bias=negm[:, 0:1],
                                                   accum_out=zb[:, 0:1]))
                        i = nc.scalar.activation(enm[:, 0:1], negm[:, 0:1], AF.Exp)
                        c.fin(i, "s", nc.scalar)
                        c.wait(nc.vector, "s")
                        if qc > 0:
                            c.sv(nc.vector.tensor_add(zb[:, 0:1], zb[:, 0:1],
                                                      za[:, 0:1]))
                        K0 = float(T - L)
                        if K0 > 0:
                            c.sv(nc.vector.scalar_tensor_tensor(
                                zb[:, 0:1], enm[:, 0:1], K0, zb[:, 0:1],
                                ALU.mult, ALU.add))
                        c.sv(nc.vector.reciprocal(rcp[:, 0:1], zb[:, 0:1]))
                        c.sv(nc.vector.tensor_scalar_mul(p_sb[:, 0:L],
                                                         p_sb[:, 0:L],
                                                         rcp[:, 0:1]))
                        i = nc.vector.tensor_mul(w0[:, 0:1], enm[:, 0:1],
                                                 rcp[:, 0:1])
                        c.fin(i, "v", nc.vector)
                        c.wait(nc.tensor, "v")
                        for tcb in range(qc + 1):
                            i = nc.tensor.transpose(ptp[:],
                                                    p_sb[:, tcb * P:(tcb + 1) * P],
                                                    cst.ident_bf[:])
                            c.inc(i, "t")
                            c.wait(nc.vector, "t")
                            i = nc.vector.tensor_copy(pT_sb[:], ptp[:])
                            c.fin(i, "v", nc.vector)
                            c.wait(nc.tensor, "v")
                            i = nc.tensor.matmul(pot[:], V[tcb][:], pT_sb[:],
                                                 start=(tcb == 0), stop=False)
                        if qc < 15:
                            i = nc.tensor.transpose(pw0[:], w0[:, 0:1],
                                                    cst.ident_f[:])
                            c.inc(i, "t")
                            c.wait(nc.vector, "t")
                            i = nc.vector.tensor_copy(w0r[:], pw0[:])
                            c.fin(i, "v", nc.vector)
                            c.wait(nc.tensor, "v")
                            i = nc.tensor.matmul(
                                pot[:], vcs[0:1, (qc + 1) * DH:(qc + 2) * DH],
                                w0r[:], start=False, stop=True)
                        else:
                            i = nc.tensor.matmul(pot[:], V[qc][:], pT_sb[:],
                                                 start=False, stop=True,
                                                 skip_group_check=True) if False \
                                else i
                        c.inc(i, "t")
                        c.wait(nc.vector, "t")
                        dst = hidT[h // 2][(h % 2) * DH:(h % 2) * DH + DH, qs]
                        i = nc.vector.tensor_copy(dst, pot[:])
                        c.fin(i, "v", nc.vector)
                        c.wait(nc.tensor, "v")
        with c.scope():
            ohT = c.sb("r_ohT", [8, T], F32)
            rpT = c.sb("r_rpT", [1, T], F32)
            wswl = c.sb("r_wswl", [P, 8 * NKD], F32)
            wsw_bf = c.sb("r_wswb", [P, 8 * NKD], BF16)
            bswr = c.sb("r_bswr", [1, 8], F32)
            bswb = c.sb("r_bswb", [P, 8], F32)
            lg = c.sb("r_lg", [P, 8], F32)
            em = c.sb("r_em", [P, 8], F32)
            mx = c.sb("r_mx", [P, 1], F32)
            ngm = c.sb("r_ngm", [P, 1], F32)
            zz = c.sb("r_zz", [P, 1], F32)
            rc2 = c.sb("r_rc2", [P, 1], F32)
            rp1 = c.sb("r_rp1", [P, 1], F32)
            oh = c.sb("r_oh", [P, 8], F32)
            cnt_sb = c.sb("r_cnt", [8, 1], F32)
            c.dma_join()
            for k in range(NKD):
                i = nc.sync.dma_start(out=wswl[:, 8 * k:8 * k + 8],
                                      in_=dram["L1_ffn_Wsw_t"][k * P:(k + 1) * P, :])
                c.inc(i, "ld", 16)
            i = nc.sync.dma_start(out=bswr[:], in_=dram["L1_ffn_bsw_row"])
            c.inc(i, "ld", 16)
            c.wait(nc.vector, "ld")
            i = nc.vector.tensor_copy(wsw_bf[:], wswl[:])
            c.fin(i, "v", nc.vector)
            c.wait(nc.tensor, "v")
            plg = c.ps("r_plg", [P, 8])
            pt8 = c.ps("r_pt8", [8, P])
            pt1 = c.ps("r_pt1", [1, P])
            i = nc.tensor.matmul(plg[:], cst.ones_row[:], bswr[:],
                                 start=True, stop=True)
            c.inc(i, "t")
            c.wait(nc.vector, "t")
            i = nc.vector.tensor_copy(bswb[:], plg[:])
            c.fin(i, "v", nc.vector)
            c.wait(nc.tensor, "v")
            for tt in range(T // P):
                ts_ = slice(tt * P, (tt + 1) * P)
                for k in range(NKD):
                    i = nc.tensor.matmul(plg[:], hidT[k][:, ts_],
                                         wsw_bf[:, 8 * k:8 * k + 8],
                                         start=(k == 0), stop=(k == NKD - 1))
                c.inc(i, "t")
                c.wait(nc.vector, "t")
                c.sv(nc.vector.tensor_tensor(lg[:], plg[:], bswb[:], ALU.add))
                c.sv(nc.vector.reduce_max(mx[:, 0:1], lg[:], axis=AX.X))
                i = nc.vector.tensor_scalar_mul(ngm[:, 0:1], mx[:, 0:1], -1.0)
                c.fin(i, "v", nc.vector)
                c.wait(nc.scalar, "v")
                i = nc.scalar.activation(em[:], lg[:], AF.Exp, bias=ngm[:, 0:1],
                                         accum_out=zz[:, 0:1])
                c.fin(i, "s", nc.scalar)
                c.wait(nc.vector, "s")
                c.sv(nc.vector.reciprocal(rc2[:, 0:1], zz[:, 0:1]))
                c.sv(nc.vector.reduce_max(rp1[:, 0:1], em[:], axis=AX.X))
                c.sv(nc.vector.tensor_mul(rp1[:, 0:1], rp1[:, 0:1], rc2[:, 0:1]))
                c.sv(nc.vector.tensor_scalar_add(lg[:], lg[:], ngm[:, 0:1]))
                i = nc.vector.tensor_single_scalar(oh[:], lg[:], 0.0, ALU.is_ge)
                c.fin(i, "v", nc.vector)
                c.wait(nc.tensor, "v")
                nc.tensor.transpose(pt8[:], oh[:], cst.ident_f[:])
                i = nc.tensor.transpose(pt1[:], rp1[:, 0:1], cst.ident_f[:])
                c.inc(i, "t")
                c.wait(nc.vector, "t")
                c.sv(nc.vector.tensor_copy(ohT[:, ts_], pt8[:]))
                i = nc.vector.tensor_copy(rpT[0:1, ts_], pt1[:])
                c.fin(i, "v", nc.vector)
                c.wait(nc.tensor, "v")
            i = nc.vector.tensor_reduce(cnt_sb[:, 0:1], ohT[:], axis=AX.X,
                                        op=ALU.add)
            c.fin(i, "v", nc.vector)
            c.wait(nc.gpsimd, "v")
            for k in range(NKD):
                i = nc.gpsimd.dma_start(out=hid_out[k * P:(k + 1) * P, :],
                                        in_=hidT[k][:])
                c.inc(i, "st", 16)
            i = nc.gpsimd.dma_start(out=oh_out, in_=ohT[:])
            c.inc(i, "st", 16)
            i = nc.gpsimd.dma_start(out=rp_out, in_=rpT[:])
            c.inc(i, "st", 16)
            i = nc.gpsimd.dma_start(out=cnt_out, in_=cnt_sb[:])
            c.inc(i, "st", 16)


def emit_l1moe(c, cst, dram, x_in, emb_in, hid_in, oh_in, rp_in, ctbl_in,
               pmask_in, x_out, capacity=4915.0, dbg=None):
    nc = c.nc
    c.phase_join()
    with c.scope():
        rpT = c.sb("m_rpT", [1, T], F32)
        ctbl = c.sb("m_ctbl", [16, 8], F32)
        pmask = c.sb("m_pm", [16, 1], F32)
        gam = c.sb("m_gam", [P, NKD], F32)
        bet = c.sb("m_bet", [P, NKD], F32)
        gam2 = c.sb("m_gam2", [P, NKD], F32)
        bet2 = c.sb("m_bet2", [P, NKD], F32)
        mdisp = c.sb("m_md", [8, T], F32)
        mdisp_bf = c.sb("m_mdb", [8, T], BF16)
        offs = c.sb("m_offs", [8, 1], F32)
        be_bf = c.sb("m_beb", [8, D], BF16)
        bel = c.sb("m_bel", [8, D], F32)
        hid = [c.sb("m_hid", [P, T], BF16) for _ in range(NKD)]
        yacc = [c.sb("m_y", [P, T], F32) for _ in range(NKD)]
        routing_scope = c.scope()
        routing_es = routing_scope.__enter__()
        ohT = c.sb("m_ohT", [8, T], F32)
        ones8 = c.sb("m_on8", [8, T], F32)
        pos = c.sb("m_pos", [8, T], F32)
        c.dma_join()
        if "st" in c.cnt:
            c.wait(nc.sync, "st")
        i = nc.sync.dma_start(out=ohT[:], in_=oh_in)
        c.inc(i, "ld", 16)
        i = nc.sync.dma_start(out=rpT[:], in_=rp_in)
        c.inc(i, "ld", 16)
        i = nc.sync.dma_start(out=ctbl[:], in_=ctbl_in)
        c.inc(i, "ld", 16)
        i = nc.sync.dma_start(out=pmask[:], in_=pmask_in)
        c.inc(i, "ld", 16)
        i = nc.sync.dma_start(out=bel[:], in_=dram["L1_ffn_be"])
        c.inc(i, "ld", 16)
        for k in range(NKD):
            i = nc.sync.dma_start(out=gam[:, k:k + 1], in_=dram["L1_ffn_gamma"][k])
            c.inc(i, "ld", 16)
            i = nc.sync.dma_start(out=bet[:, k:k + 1], in_=dram["L1_ffn_beta"][k])
            c.inc(i, "ld", 16)
        ldh = load_tiles(c, hid, hid_in, NKD, T)
        c.wait(nc.vector, "ld", ldh)
        c.sv(nc.vector.tensor_scalar_mul(gam2[:], gam[:], 0.5))
        c.sv(nc.vector.tensor_scalar_mul(bet2[:], bet[:], 0.5))
        c.sv(nc.vector.tensor_copy(be_bf[:], bel[:]))
        c.sv(nc.vector.memset(ones8[:], 1.0))
        i = nc.vector.engine_nop()
        c.fin(i, "v", nc.vector)
        poff = c.ps("m_poff", [8, 1])
        c.wait(nc.tensor, "v")
        i = nc.tensor.matmul(poff[:], ctbl[:], pmask[:], start=True, stop=True)
        c.inc(i, "t")
        c.wait(nc.vector, "t")
        c.sv(nc.vector.tensor_copy(offs[:], poff[:]))
        c.sv(nc.vector.tensor_tensor_scan(pos[:], ones8[:], ohT[:],
                                          offs[:, 0:1], ALU.mult, ALU.add))
        c.sv(nc.vector.tensor_single_scalar(pos[:], pos[:], capacity + 0.5,
                                            ALU.is_le))
        c.sv(nc.vector.tensor_mul(mdisp[:], pos[:], ohT[:]))
        i = nc.vector.tensor_copy(mdisp_bf[:], mdisp[:])
        c.fin(i, "v", nc.vector)
        routing_scope.__exit__(None, None, None)
        with c.scope():
            wl = [c.sb("m_wl", [P, D], F32) for _ in range(2)]
            we_bf = [c.sb("m_web", [P, D], BF16) for _ in range(NKD)]
            mh = [c.sb("m_mh", [P, FC], BF16) for _ in range(NKD)]
            pmb = c.ps("m_pmb", [P, FC])
            pyy = [c.ps("m_pyy", [P, FC]) for _ in range(2)]
            py_free = [0, 0]
            cd = []
            c.dma_join()
            for e in range(8):
                for k in range(NKD):
                    if len(cd) >= 2:
                        c.wait(nc.sync, "v", cd[-2])
                    i = nc.sync.dma_start(
                        out=wl[len(cd) % 2][:],
                        in_=dram["L1_ffn_We_t"][e, k * P:(k + 1) * P, :])
                    c.inc(i, "ld", 16)
                    c.wait(nc.vector, "ld")
                    i = nc.vector.tensor_copy(we_bf[k][:], wl[len(cd) % 2][:])
                    cd.append(c.inc(i, "v"))
                c.wait(nc.tensor, "v")
                for f in range(NFC):
                    fs = slice(f * FC, (f + 1) * FC)
                    i = nc.tensor.matmul(pmb[:], cst.sel8[:, e * P:(e + 1) * P],
                                         mdisp[:, fs], start=True, stop=True)
                    c.inc(i, "t")
                    c.wait(nc.vector, "t")
                    for k in range(NKD):
                        i = nc.vector.tensor_tensor(mh[k][:], hid[k][:, fs],
                                                    pmb[:], ALU.mult)
                    c.fin(i, "v", nc.vector)
                    c.wait(nc.tensor, "v")
                    for mo in range(NKD):
                        mos = slice(mo * P, (mo + 1) * P)
                        pyt = pyy[mo % 2]
                        c.wait(nc.tensor, "v", py_free[mo % 2])
                        for k in range(NKD):
                            i = nc.tensor.matmul(pyt[:], we_bf[k][:, mos],
                                                 mh[k][:], start=(k == 0),
                                                 stop=(e > 0 and k == NKD - 1))
                        if e == 0:
                            i = nc.tensor.matmul(pyt[:], be_bf[:, mos],
                                                 mdisp_bf[:, fs],
                                                 start=False, stop=True)
                        c.inc(i, "t")
                        c.wait(nc.vector, "t")
                        if e == 0:
                            i = nc.vector.tensor_copy(yacc[mo][:, fs], pyt[:])
                        else:
                            i = nc.vector.tensor_add(yacc[mo][:, fs],
                                                     yacc[mo][:, fs], pyt[:])
                        py_free[mo % 2] = c.fin_val(i, "v", nc.vector)
                    c.wait(nc.tensor, "v")
        if dbg:
            c.wait(nc.gpsimd, "v")
            for k in range(NKD):
                i = nc.gpsimd.dma_start(out=dbg["yacc"][k * P:(k + 1) * P, :],
                                        in_=yacc[k][:])
                c.inc(i, "st", 16)
            i = nc.gpsimd.dma_start(out=dbg["md"], in_=mdisp[:])
            c.inc(i, "st", 16)
            c.wait(nc.tensor, "st")
            c.wait(nc.vector, "st")
        FE = 256
        with c.scope():
            notk = c.sb("m_ntk", [P, FE], F32)
            tmpa = c.sb("m_tma", [P, FE], F32)
            x2 = [c.sb("m_x2", [P, FE], F32) for _ in range(NKD)]
            emb = [c.sb("m_emb", [P, FE], F32) for _ in range(NKD)]
            x3 = [c.sb("m_x3", [P, FE], F32) for _ in range(NKD)]
            lno = emb
            pkb = c.ps("m_pkb", [P, FE])
            prp = c.ps("m_prp", [P, FE])
            for f in range(T // FE):
                fs = slice(f * FE, (f + 1) * FE)
                c.dma_join()
                if "st" in c.cnt:
                    c.wait(nc.sync, "st")
                for k in range(NKD):
                    i = nc.sync.dma_start(out=emb[k][:],
                                          in_=emb_in[k * P:(k + 1) * P, fs])
                    c.inc(i, "ld", 16)
                    i = nc.sync.dma_start(out=x3[k][:],
                                          in_=x_in[k * P:(k + 1) * P, fs])
                    c.inc(i, "ld", 16)
                ldc = c.cnt["ld"]
                c.wait(nc.tensor, "ld", ldc)
                nc.tensor.matmul(pkb[:], cst.allon8[:], mdisp[:, fs],
                                 start=True, stop=True)
                i = nc.tensor.matmul(prp[:], cst.ones_row[:], rpT[0:1, fs],
                                     start=True, stop=True)
                c.inc(i, "t")
                for eng in (nc.vector, nc.scalar):
                    c.wait(eng, "ld", ldc)
                c.wait(nc.vector, "t")
                c.sv(nc.vector.tensor_scalar(notk[:], pkb[:], -1.0, 1.0,
                                             ALU.mult, ALU.add))
                for k in range(NKD):
                    c.sv(nc.vector.tensor_tensor(tmpa[:], hid[k][:, fs],
                                                 notk[:], ALU.mult))
                    c.sv(nc.vector.tensor_add(tmpa[:], tmpa[:], yacc[k][:, fs]))
                    c.sv(nc.vector.tensor_tensor(tmpa[:], tmpa[:], prp[:],
                                                 ALU.mult))
                    i = nc.vector.tensor_add(x2[k][:], tmpa[:], emb[k][:])
                c.fin(i, "v", nc.vector)
                c.wait(nc.scalar, "v")
                c.wait(nc.tensor, "v")
                emit_ln(c, cst, x2, None, FE,
                        gamma=[gam2[:, k:k + 1] for k in range(NKD)],
                        beta=[bet2[:, k:k + 1] for k in range(NKD)],
                        out_f32=lno)
                for k in range(NKD):
                    i = nc.vector.tensor_add(lno[k][:], lno[k][:], x3[k][:])
                c.fin(i, "v", nc.vector)
                c.wait(nc.gpsimd, "v")
                for k in range(NKD):
                    i = nc.gpsimd.dma_start(out=x_out[k * P:(k + 1) * P, fs],
                                            in_=lno[k][:])
                    c.inc(i, "st", 16)


def emit_l1attn(c, cst, dram, x_in, x_out, dbg=None):
    nc = c.nc
    c.phase_join()
    H, DH = 8, 128
    with c.scope():
        xnb = [c.sb("a_xn", [P, T], BF16) for _ in range(NKD)]
        xnl = [c.sb("a_xnl", [P, T], BF16) for _ in range(NKD)]
        bq = c.sb("a_bq", [P, H], F32)
        bo = c.sb("a_bo", [P, NKD], F32)
        bkr = c.sb("a_bkr", [1, P * H], F32)
        bvr = c.sb("a_bvr", [1, P * H], F32)
        with c.scope():
            x_sb = [c.sb("a_x", [P, T], F32) for _ in range(NKD)]
            c.dma_join()
            if "st" in c.cnt:
                c.wait(nc.sync, "st")
            ldx = load_tiles(c, x_sb, x_in, NKD, T)
            for h in range(H):
                i = nc.sync.dma_start(out=bq[:, h:h + 1], in_=dram["L1_attn_bq"][h])
                c.inc(i, "ld", 16)
                i = nc.sync.dma_start(out=bkr[:, h * P:(h + 1) * P],
                                      in_=dram["L1_attn_bk_row"][h])
                c.inc(i, "ld", 16)
                i = nc.sync.dma_start(out=bvr[:, h * P:(h + 1) * P],
                                      in_=dram["L1_attn_bv_row"][h])
                c.inc(i, "ld", 16)
            for k in range(NKD):
                i = nc.sync.dma_start(out=bo[:, k:k + 1], in_=dram["L1_attn_bo"][k])
                c.inc(i, "ld", 16)
            for eng in (nc.tensor, nc.vector, nc.scalar):
                c.wait(eng, "ld")
            emit_ln(c, cst, x_sb, xnb, T, out_lo=xnl)
        outTh = [c.sb("a_oTh", [P, T], BF16) for _ in range(NKD)]
        outTl = [c.sb("a_oTl", [P, T], BF16) for _ in range(NKD)]
        with c.scope():
            qTh = c.sb("a_qTh", [P, T], BF16)
            qTl = c.sb("a_qTl", [P, T], BF16)
            Kh = [c.sb("a_Kh", [P, DH], BF16) for _ in range(T // P)]
            Kl = [c.sb("a_Kl", [P, DH], BF16) for _ in range(T // P)]
            Vh = [c.sb("a_Vh", [P, DH], BF16) for _ in range(T // P)]
            Vl = [c.sb("a_Vl", [P, DH], BF16) for _ in range(T // P)]
            Sh = c.sb("a_Sh", [P, P], BF16)
            Sl = c.sb("a_Sl", [P, P], BF16)
            bkb = c.sb("a_bkb", [P, P], F32)
            bvb = c.sb("a_bvb", [P, P], F32)
            tf = c.sb("a_tf", [P, FC], F32)
            tu = c.sb("a_tu", [P, FC], F32)
            wl = [c.sb("a_wl", [P, DH], F32) for _ in range(2)]
            wqh = [c.sb("a_wqh", [P, DH], BF16) for _ in range(NKD)]
            wql = [c.sb("a_wql", [P, DH], BF16) for _ in range(NKD)]
            wkh = [c.sb("a_wkh", [P, DH], BF16) for _ in range(NKD)]
            wkl = [c.sb("a_wkl", [P, DH], BF16) for _ in range(NKD)]
            wvh = [c.sb("a_wvh", [P, DH], BF16) for _ in range(NKD)]
            wvl = [c.sb("a_wvl", [P, DH], BF16) for _ in range(NKD)]
            cd = []

            def stream(hi_list, lo_list, src3):
                for k in range(NKD):
                    if len(cd) >= 2:
                        c.wait(nc.sync, "v", cd[-2])
                    i = nc.sync.dma_start(out=wl[len(cd) % 2][:],
                                          in_=src3[k * P:(k + 1) * P, :])
                    c.inc(i, "ld", 16)
                    c.wait(nc.vector, "ld")
                    c.sv(nc.vector.tensor_copy(hi_list[k][:],
                                               wl[len(cd) % 2][:]))
                    c.sv(nc.vector.tensor_copy(tu[:, 0:DH], hi_list[k][:]))
                    c.sv(nc.vector.tensor_sub(tu[:, 0:DH], wl[len(cd) % 2][:],
                                              tu[:, 0:DH]))
                    i = nc.vector.tensor_copy(lo_list[k][:], tu[:, 0:DH])
                    cd.append(c.inc(i, "v"))

            def split_evac(pf_ap, hi_ap, lo_ap, flen):
                c.sv(nc.vector.tensor_copy(hi_ap, tf[:, 0:flen]))
                c.sv(nc.vector.tensor_copy(tu[:, 0:flen], hi_ap))
                c.sv(nc.vector.tensor_sub(tu[:, 0:flen], tf[:, 0:flen],
                                          tu[:, 0:flen]))
                i = nc.vector.tensor_copy(lo_ap, tu[:, 0:flen])
                return i
            c.dma_join()
            for h in range(H):
                with c.scope():
                    pq = c.ps("a_pq", [P, FC])
                    pkv = c.ps("a_pkv", [P, DH])
                    pS = c.ps("a_pS", [P, P])
                    stream(wqh, wql, dram["L1_attn_wq"][h])
                    stream(wkh, wkl, dram["L1_attn_wk"][h])
                    stream(wvh, wvl, dram["L1_attn_wv"][h])
                    c.wait(nc.tensor, "v")
                    for f in range(NFC):
                        fs = slice(f * FC, (f + 1) * FC)
                        for k in range(NKD):
                            nc.tensor.matmul(pq[:], wqh[k][:], xnb[k][:, fs],
                                             start=(k == 0), stop=False)
                            nc.tensor.matmul(pq[:], wql[k][:], xnb[k][:, fs],
                                             start=False, stop=False)
                            i = nc.tensor.matmul(pq[:], wqh[k][:],
                                                 xnl[k][:, fs], start=False,
                                                 stop=(k == NKD - 1))
                        c.inc(i, "t")
                        c.wait(nc.scalar, "t")
                        i = nc.scalar.activation(tf[:], pq[:], AF.Identity,
                                                 bias=bq[:, h:h + 1])
                        c.fin(i, "s", nc.scalar)
                        c.wait(nc.vector, "s")
                        i = split_evac(None, qTh[:, fs], qTl[:, fs], FC)
                        c.fin(i, "v", nc.vector)
                        c.wait(nc.tensor, "v")
                    nc.tensor.matmul(pq[:, 0:P], cst.ones_row[:],
                                     bkr[0:1, h * P:(h + 1) * P],
                                     start=True, stop=True)
                    i = nc.tensor.matmul(pkv[:], cst.ones_row[:],
                                         bvr[0:1, h * P:(h + 1) * P],
                                         start=True, stop=True)
                    c.inc(i, "t")
                    c.wait(nc.vector, "t")
                    c.sv(nc.vector.tensor_copy(bkb[:], pq[:, 0:P]))
                    i = nc.vector.tensor_copy(bvb[:], pkv[:])
                    c.fin(i, "v", nc.vector)
                    c.wait(nc.tensor, "v")
                    for tc in range(T // P):
                        tcs = slice(tc * P, (tc + 1) * P)
                        for which, wh_, wl_, Kh_, Kl_, bb in (
                                (0, wkh, wkl, Kh, Kl, bkb),
                                (1, wvh, wvl, Vh, Vl, bvb)):
                            for k in range(NKD):
                                nc.tensor.matmul(pkv[:], xnb[k][:, tcs],
                                                 wh_[k][:], start=(k == 0),
                                                 stop=False)
                                nc.tensor.matmul(pkv[:], xnl[k][:, tcs],
                                                 wh_[k][:], start=False,
                                                 stop=False)
                                i = nc.tensor.matmul(pkv[:], xnb[k][:, tcs],
                                                     wl_[k][:], start=False,
                                                     stop=(k == NKD - 1))
                            c.inc(i, "t")
                            c.wait(nc.vector, "t")
                            i = nc.vector.tensor_tensor(tf[:, 0:DH], pkv[:],
                                                        bb[:], ALU.add)
                            c.sv(i)
                            i = split_evac(None, Kh_[tc][:], Kl_[tc][:], DH)
                            c.fin(i, "v", nc.vector)
                            c.wait(nc.tensor, "v")
                    for tc in range(T // P):
                        nc.tensor.matmul(pS[:], Kh[tc][:], Vh[tc][:],
                                         start=(tc == 0), stop=False)
                        nc.tensor.matmul(pS[:], Kl[tc][:], Vh[tc][:],
                                         start=False, stop=False)
                        i = nc.tensor.matmul(pS[:], Kh[tc][:], Vl[tc][:],
                                             start=False,
                                             stop=(tc == T // P - 1))
                    c.inc(i, "t")
                    c.wait(nc.vector, "t")
                    c.sv(nc.vector.tensor_copy(tf[:, 0:P], pS[:]))
                    i = split_evac(None, Sh[:], Sl[:], P)
                    c.fin(i, "v", nc.vector)
                    c.wait(nc.tensor, "v")
                    if dbg and h == 0:
                        c.wait(nc.gpsimd, "v")
                        i = nc.gpsimd.dma_start(out=dbg["S0"], in_=Sh[:])
                        c.inc(i, "st", 16)
                        i = nc.gpsimd.dma_start(out=dbg["qT0"], in_=qTh[:])
                        c.inc(i, "st", 16)
                        i = nc.gpsimd.dma_start(out=dbg["K0"], in_=Kh[0][:])
                        c.inc(i, "st", 16)
                        c.wait(nc.tensor, "st")
                    for f in range(NFC):
                        fs = slice(f * FC, (f + 1) * FC)
                        nc.tensor.matmul(pq[:], Sh[:], qTh[:, fs],
                                         start=True, stop=False)
                        nc.tensor.matmul(pq[:], Sl[:], qTh[:, fs],
                                         start=False, stop=False)
                        i = nc.tensor.matmul(pq[:], Sh[:], qTl[:, fs],
                                             start=False, stop=True)
                        c.inc(i, "t")
                        c.wait(nc.vector, "t")
                        c.sv(nc.vector.tensor_copy(tf[:], pq[:]))
                        i = split_evac(None, outTh[h][:, fs], outTl[h][:, fs],
                                       FC)
                        c.fin(i, "v", nc.vector)
                        c.wait(nc.tensor, "v")
        with c.scope():
            wl2 = [c.sb("a_wl2", [P, D], F32) for _ in range(2)]
            woh = [c.sb("a_woh", [P, D], BF16) for _ in range(NKD)]
            wol = [c.sb("a_wol", [P, D], BF16) for _ in range(NKD)]
            tmpo = c.sb("a_tmpo", [P, FC], F32)
            tupo = c.sb("a_tupo", [P, D], F32)
            xres = c.sb("a_xres", [P, FC], F32)
            cd2 = []
            c.dma_join()
            for k in range(NKD):
                if len(cd2) >= 2:
                    c.wait(nc.sync, "v", cd2[-2])
                i = nc.sync.dma_start(out=wl2[len(cd2) % 2][:],
                                      in_=dram["L1_attn_wo_t"][k * P:(k + 1) * P, :])
                c.inc(i, "ld", 16)
                c.wait(nc.vector, "ld")
                c.sv(nc.vector.tensor_copy(woh[k][:], wl2[len(cd2) % 2][:]))
                c.sv(nc.vector.tensor_copy(tupo[:], woh[k][:]))
                c.sv(nc.vector.tensor_sub(tupo[:], wl2[len(cd2) % 2][:],
                                          tupo[:]))
                i = nc.vector.tensor_copy(wol[k][:], tupo[:])
                cd2.append(c.inc(i, "v"))
            c.wait(nc.tensor, "v")
            po = c.ps("a_po", [P, FC])
            for f in range(NFC):
                fs = slice(f * FC, (f + 1) * FC)
                for mo in range(NKD):
                    mos = slice(mo * P, (mo + 1) * P)
                    c.dma_join(names=("v",))
                    i = nc.sync.dma_start(out=xres[:],
                                          in_=x_in[mo * P:(mo + 1) * P, fs])
                    c.inc(i, "ld", 16)
                    for k in range(NKD):
                        nc.tensor.matmul(po[:], woh[k][:, mos], outTh[k][:, fs],
                                         start=(k == 0), stop=False)
                        nc.tensor.matmul(po[:], wol[k][:, mos], outTh[k][:, fs],
                                         start=False, stop=False)
                        i = nc.tensor.matmul(po[:], woh[k][:, mos],
                                             outTl[k][:, fs], start=False,
                                             stop=(k == NKD - 1))
                    c.inc(i, "t")
                    c.wait(nc.scalar, "t")
                    i = nc.scalar.activation(tmpo[:], po[:], AF.Identity,
                                             bias=bo[:, mo:mo + 1])
                    c.fin(i, "s", nc.scalar)
                    c.wait(nc.vector, "s")
                    c.wait(nc.vector, "ld")
                    i = nc.vector.tensor_add(xres[:], xres[:], tmpo[:])
                    c.fin(i, "v", nc.vector)
                    c.wait(nc.gpsimd, "v")
                    i = nc.gpsimd.dma_start(out=x_out[mo * P:(mo + 1) * P, fs],
                                            in_=xres[:])
                    c.inc(i, "st", 16)
                    c.wait(nc.tensor, "v")
                    c.wait(nc.sync, "st")
"""Driver appended to mk.py to form kernel.py (multi-launch version)."""

from concourse.bass_utils import run_bass_kernel_spmd

NB = 2
NCORE = 8

_CONSTS = const_inputs()


def _cin_specs():
    return {k: (v.shape, BF16 if v.dtype.name == "bfloat16" else F32)
            for k, v in _CONSTS.items()}


_XS = ((NB, 1024, 2048), F32)


def _prog_ffn0():
    ins = {
        "xin": _XS,
        "L0_ffn_w_gate": ((1024, 8), F32),
        "L0_ffn_w_g": ((8, 1024, 1408), F32),
        "L0_ffn_w_u": ((8, 1024, 1408), F32),
        "L0_ffn_w_d": ((8, 1408, 1024), F32),
    }
    ins.update(_cin_specs())
    outs = {"xout": _XS}

    def phase(c, cst, dram):
        for b in range(NB):
            emit_ffn0(c, cst, dram, dram["xin"][b], dram["xout"][b])
    return build_program([phase], ins, outs, consts_need=("sel8", "ident_f"))


def _prog_mamba():
    ins = {
        "xin": _XS,
        "L0_attn_ln_g": ((8, 128, 1), F32),
        "L0_attn_ln_b": ((8, 128, 1), F32),
        "L0_attn_W_in": ((1024, 4096), F32),
        "L0_attn_conv_w": ((16, 128, 4), F32),
        "L0_attn_conv_b": ((16, 128, 1), F32),
        "L0_attn_W_b": ((2048, 16), F32),
        "L0_attn_W_c": ((2048, 16), F32),
        "L0_attn_W_delta": ((2048, 2048), F32),
        "L0_attn_b_delta": ((16, 128, 1), F32),
        "L0_attn_log_A": ((16, 128, 16), F32),
        "L0_attn_Dp": ((16, 128, 1), F32),
        "L0_attn_W_out": ((2048, 1024), F32),
    }
    ins.update(_cin_specs())
    outs = {"xout": _XS}

    def phase(c, cst, dram):
        for b in range(NB):
            emit_mamba(c, cst, dram, dram["xin"][b], dram["xout"][b])
    return build_program([phase], ins, outs, consts_need=("sel16",))


def _prog_router():
    ins = {
        "xin": _XS,
        "L1_ffn_Wq": ((16, 1024, 64), F32),
        "L1_ffn_Wk": ((16, 1024, 64), F32),
        "L1_ffn_Wv": ((16, 1024, 64), F32),
        "L1_ffn_bq": ((16, 64, 1), F32),
        "L1_ffn_bk": ((16, 64, 1), F32),
        "L1_ffn_bv": ((16, 1, 64), F32),
        "L1_ffn_Wsw_t": ((1024, 8), F32),
        "L1_ffn_bsw_row": ((1, 8), F32),
    }
    ins.update(_cin_specs())
    outs = {"hid": ((NB, 1024, 2048), BF16), "emb": _XS,
            "oh": ((NB, 8, 2048), F32), "rp": ((NB, 1, 2048), F32),
            "cnt": ((NB, 8, 1), F32)}

    def phase(c, cst, dram):
        for b in range(NB):
            emit_l1router(c, cst, dram, dram["xin"][b], dram["hid"][b],
                          dram["emb"][b], dram["oh"][b], dram["rp"][b],
                          dram["cnt"][b])
    return build_program([phase], ins, outs,
                         consts_need=("ident_bf", "ident_f", "tril_f"))


def _prog_moe():
    ins = {
        "xin": _XS,
        "emb": _XS,
        "hid": ((NB, 1024, 2048), BF16),
        "oh": ((NB, 8, 2048), F32),
        "rp": ((NB, 1, 2048), F32),
        "ctbl": ((16, 8), F32),
        "pmask": ((NB, 16, 1), F32),
        "L1_ffn_We_t": ((8, 1024, 1024), F32),
        "L1_ffn_be": ((8, 1024), F32),
        "L1_ffn_gamma": ((8, 128, 1), F32),
        "L1_ffn_beta": ((8, 128, 1), F32),
    }
    ins.update(_cin_specs())
    outs = {"xout": _XS}

    def phase(c, cst, dram):
        for b in range(NB):
            emit_l1moe(c, cst, dram, dram["xin"][b], dram["emb"][b],
                       dram["hid"][b], dram["oh"][b], dram["rp"][b],
                       dram["ctbl"], dram["pmask"][b], dram["xout"][b])
    return build_program([phase], ins, outs, consts_need=("sel8", "allon8"))


def _prog_attn():
    ins = {
        "xin": _XS,
        "L1_attn_wq": ((8, 1024, 128), F32),
        "L1_attn_wk": ((8, 1024, 128), F32),
        "L1_attn_wv": ((8, 1024, 128), F32),
        "L1_attn_bq": ((8, 128, 1), F32),
        "L1_attn_bk_row": ((8, 1, 128), F32),
        "L1_attn_bv_row": ((8, 1, 128), F32),
        "L1_attn_wo_t": ((1024, 1024), F32),
        "L1_attn_bo": ((8, 128, 1), F32),
    }
    ins.update(_cin_specs())
    outs = {"xout": _XS}

    def phase(c, cst, dram):
        for b in range(NB):
            emit_l1attn(c, cst, dram, dram["xin"][b], dram["xout"][b])
    return build_program([phase], ins, outs, consts_need=())


_PROGS = {}


def _prog(name):
    if name not in _PROGS:
        _PROGS[name] = {"ffn0": _prog_ffn0, "mamba": _prog_mamba,
                        "router": _prog_router, "moe": _prog_moe,
                        "attn": _prog_attn}[name]()
    return _PROGS[name]


def _layout_weights(inp):
    W = {}
    W["L0_ffn_w_gate"] = np.ascontiguousarray(inp["L0_ffn_w_gate"])
    for k in ("L0_ffn_w_g", "L0_ffn_w_u", "L0_ffn_w_d"):
        W[k] = np.ascontiguousarray(inp[k])
    W["L0_attn_ln_g"] = np.ascontiguousarray(inp["L0_attn_ln_g"]).reshape(8, 128, 1)
    W["L0_attn_ln_b"] = np.ascontiguousarray(inp["L0_attn_ln_b"]).reshape(8, 128, 1)
    W["L0_attn_W_in"] = np.ascontiguousarray(inp["L0_attn_W_in"])
    W["L0_attn_conv_w"] = np.ascontiguousarray(inp["L0_attn_conv_w"]).reshape(16, 128, 4)
    W["L0_attn_conv_b"] = np.ascontiguousarray(inp["L0_attn_conv_b"]).reshape(16, 128, 1)
    W["L0_attn_W_b"] = np.ascontiguousarray(inp["L0_attn_W_b"])
    W["L0_attn_W_c"] = np.ascontiguousarray(inp["L0_attn_W_c"])
    W["L0_attn_W_delta"] = np.ascontiguousarray(inp["L0_attn_W_delta"])
    W["L0_attn_b_delta"] = np.ascontiguousarray(inp["L0_attn_b_delta"]).reshape(16, 128, 1)
    W["L0_attn_log_A"] = np.ascontiguousarray(inp["L0_attn_log_A"]).reshape(16, 128, 16)
    W["L0_attn_Dp"] = np.ascontiguousarray(inp["L0_attn_Dp"]).reshape(16, 128, 1)
    W["L0_attn_W_out"] = np.ascontiguousarray(inp["L0_attn_W_out"])
    W["L1_ffn_Wq"] = np.ascontiguousarray(inp["L1_ffn_Wq"].transpose(0, 2, 1))
    W["L1_ffn_Wk"] = np.ascontiguousarray(inp["L1_ffn_Wk"].transpose(0, 2, 1))
    W["L1_ffn_Wv"] = np.ascontiguousarray(inp["L1_ffn_Wv"].transpose(0, 2, 1))
    W["L1_ffn_bq"] = np.ascontiguousarray(inp["L1_ffn_bq"])[:, :, None]
    W["L1_ffn_bk"] = np.ascontiguousarray(inp["L1_ffn_bk"])[:, :, None]
    W["L1_ffn_bv"] = np.ascontiguousarray(inp["L1_ffn_bv"])[:, None, :]
    W["L1_ffn_Wsw_t"] = np.ascontiguousarray(inp["L1_ffn_Wsw"].T)
    W["L1_ffn_bsw_row"] = np.ascontiguousarray(inp["L1_ffn_bsw"])[None, :]
    W["L1_ffn_We_t"] = np.ascontiguousarray(inp["L1_ffn_We"].transpose(0, 2, 1))
    W["L1_ffn_be"] = np.ascontiguousarray(inp["L1_ffn_be"])
    W["L1_ffn_gamma"] = np.ascontiguousarray(inp["L1_ffn_gamma"]).reshape(8, 128, 1)
    W["L1_ffn_beta"] = np.ascontiguousarray(inp["L1_ffn_beta"]).reshape(8, 128, 1)
    w_qkv = np.asarray(inp["L1_attn_w_qkv"])
    b_qkv = np.asarray(inp["L1_attn_b_qkv"])
    wq_s = np.empty((8, 1024, 128), np.float32)
    wk_s = np.empty((8, 1024, 128), np.float32)
    wv_s = np.empty((8, 1024, 128), np.float32)
    bq_s = np.empty((8, 128, 1), np.float32)
    bk_r = np.empty((8, 1, 128), np.float32)
    bv_r = np.empty((8, 1, 128), np.float32)
    for h in range(8):
        rows = np.arange(128) * 8 + h
        wq_s[h] = w_qkv[rows].T
        wk_s[h] = w_qkv[1024 + rows].T
        wv_s[h] = w_qkv[2048 + rows].T
        bq_s[h, :, 0] = b_qkv[rows]
        bk_r[h, 0] = b_qkv[1024 + rows]
        bv_r[h, 0] = b_qkv[2048 + rows]
    W["L1_attn_wq"], W["L1_attn_wk"], W["L1_attn_wv"] = wq_s, wk_s, wv_s
    W["L1_attn_bq"], W["L1_attn_bk_row"], W["L1_attn_bv_row"] = bq_s, bk_r, bv_r
    W["L1_attn_wo_t"] = np.ascontiguousarray(np.asarray(inp["L1_attn_w_o"]).T)
    W["L1_attn_bo"] = np.ascontiguousarray(inp["L1_attn_b_o"]).reshape(8, 128, 1)
    W.update(_CONSTS)
    return W


def _launch(name, per_core):
    nc = _prog(name)
    res = run_bass_kernel_spmd(nc, per_core, core_ids=list(range(NCORE)))
    return [r for r in res.results]


def kernel(**inputs):
    W = _layout_weights(inputs)
    x = np.ascontiguousarray(
        np.asarray(inputs["L0_attn_x"]).transpose(0, 2, 1)).astype(np.float32)
    shards = [np.ascontiguousarray(x[2 * i:2 * i + 2]) for i in range(NCORE)]

    def maps(extra_keys, per_core_extra=None):
        out = []
        for i in range(NCORE):
            m = {k: W[k] for k in extra_keys if k in W}
            m.update(_CONSTS)
            if per_core_extra is not None:
                m.update(per_core_extra[i])
            out.append(m)
        return out

    fk = ("L0_ffn_w_gate", "L0_ffn_w_g", "L0_ffn_w_u", "L0_ffn_w_d")
    mk_ = tuple(k for k in W if k.startswith("L0_attn_"))
    rk = ("L1_ffn_Wq", "L1_ffn_Wk", "L1_ffn_Wv", "L1_ffn_bq", "L1_ffn_bk",
          "L1_ffn_bv", "L1_ffn_Wsw_t", "L1_ffn_bsw_row")
    mok = ("L1_ffn_We_t", "L1_ffn_be", "L1_ffn_gamma", "L1_ffn_beta")
    ak = tuple(k for k in W if k.startswith("L1_attn_"))

    r = _launch("ffn0", maps(fk, [{"xin": shards[i]} for i in range(NCORE)]))
    x1 = [r[i]["xout"] for i in range(NCORE)]
    r = _launch("mamba", maps(mk_, [{"xin": x1[i]} for i in range(NCORE)]))
    x2 = [r[i]["xout"] for i in range(NCORE)]
    r = _launch("ffn0", maps(fk, [{"xin": x2[i]} for i in range(NCORE)]))
    x3 = [r[i]["xout"] for i in range(NCORE)]

    def l1ffn(xc):
        r = _launch("router", maps(rk, [{"xin": xc[i]} for i in range(NCORE)]))
        ctbl = np.concatenate([r[i]["cnt"][:, :, 0] for i in range(NCORE)],
                              axis=0).astype(np.float32)
        pm = np.tril(np.ones((16, 16), np.float32), -1)
        per = []
        for i in range(NCORE):
            per.append({
                "xin": xc[i], "emb": r[i]["emb"], "hid": r[i]["hid"],
                "oh": r[i]["oh"], "rp": r[i]["rp"],
                "ctbl": np.ascontiguousarray(ctbl),
                "pmask": np.ascontiguousarray(
                    pm[[2 * i, 2 * i + 1]][:, :, None]),
            })
        r2 = _launch("moe", maps(mok, per))
        return [r2[i]["xout"] for i in range(NCORE)]

    x4 = l1ffn(x3)
    r = _launch("attn", maps(ak, [{"xin": x4[i]} for i in range(NCORE)]))
    x5 = [r[i]["xout"] for i in range(NCORE)]
    x6 = l1ffn(x5)

    full = np.concatenate(x6, axis=0)
    return np.ascontiguousarray(full.transpose(0, 2, 1)).astype(np.float32)


def _prog_l0():
    ins = {
        "xin": _XS,
        "L0_ffn_w_gate": ((1024, 8), F32),
        "L0_ffn_w_g": ((8, 1024, 1408), F32),
        "L0_ffn_w_u": ((8, 1024, 1408), F32),
        "L0_ffn_w_d": ((8, 1408, 1024), F32),
        "L0_attn_ln_g": ((8, 128, 1), F32),
        "L0_attn_ln_b": ((8, 128, 1), F32),
        "L0_attn_W_in": ((1024, 4096), F32),
        "L0_attn_conv_w": ((16, 128, 4), F32),
        "L0_attn_conv_b": ((16, 128, 1), F32),
        "L0_attn_W_b": ((2048, 16), F32),
        "L0_attn_W_c": ((2048, 16), F32),
        "L0_attn_W_delta": ((2048, 2048), F32),
        "L0_attn_b_delta": ((16, 128, 1), F32),
        "L0_attn_log_A": ((16, 128, 16), F32),
        "L0_attn_Dp": ((16, 128, 1), F32),
        "L0_attn_W_out": ((2048, 1024), F32),
        "L1_ffn_Wq": ((16, 1024, 64), F32),
        "L1_ffn_Wk": ((16, 1024, 64), F32),
        "L1_ffn_Wv": ((16, 1024, 64), F32),
        "L1_ffn_bq": ((16, 64, 1), F32),
        "L1_ffn_bk": ((16, 64, 1), F32),
        "L1_ffn_bv": ((16, 1, 64), F32),
        "L1_ffn_Wsw_t": ((1024, 8), F32),
        "L1_ffn_bsw_row": ((1, 8), F32),
    }
    ins.update(_cin_specs())
    outs = {"x3": _XS, "hid": ((NB, 1024, 2048), BF16), "emb": _XS,
            "oh": ((NB, 8, 2048), F32), "rp": ((NB, 1, 2048), F32),
            "cnt": ((NB, 8, 1), F32)}
    scr = {"t1": ((NB, 1024, 2048), F32), "t2": ((NB, 1024, 2048), F32)}

    def phase(c, cst, dram):
        for b in range(NB):
            emit_ffn0(c, cst, dram, dram["xin"][b], dram["t1"][b])
            emit_mamba(c, cst, dram, dram["t1"][b], dram["t2"][b])
            emit_ffn0(c, cst, dram, dram["t2"][b], dram["x3"][b])
            emit_l1router(c, cst, dram, dram["x3"][b], dram["hid"][b],
                          dram["emb"][b], dram["oh"][b], dram["rp"][b],
                          dram["cnt"][b])
    return build_program([phase], ins, outs,
                         consts_need=("sel8", "sel16", "ident_bf", "ident_f",
                                      "tril_f"),
                         scratch_specs=scr)


def _prog_mid():
    ins = {
        "x3": _XS,
        "emb": _XS,
        "hid": ((NB, 1024, 2048), BF16),
        "oh": ((NB, 8, 2048), F32),
        "rp": ((NB, 1, 2048), F32),
        "ctbl": ((16, 8), F32),
        "pmask": ((NB, 16, 1), F32),
        "L1_ffn_We_t": ((8, 1024, 1024), F32),
        "L1_ffn_be": ((8, 1024), F32),
        "L1_ffn_gamma": ((8, 128, 1), F32),
        "L1_ffn_beta": ((8, 128, 1), F32),
        "L1_attn_wq": ((8, 1024, 128), F32),
        "L1_attn_wk": ((8, 1024, 128), F32),
        "L1_attn_wv": ((8, 1024, 128), F32),
        "L1_attn_bq": ((8, 128, 1), F32),
        "L1_attn_bk_row": ((8, 1, 128), F32),
        "L1_attn_bv_row": ((8, 1, 128), F32),
        "L1_attn_wo_t": ((1024, 1024), F32),
        "L1_attn_bo": ((8, 128, 1), F32),
        "L1_ffn_Wq": ((16, 1024, 64), F32),
        "L1_ffn_Wk": ((16, 1024, 64), F32),
        "L1_ffn_Wv": ((16, 1024, 64), F32),
        "L1_ffn_bq": ((16, 64, 1), F32),
        "L1_ffn_bk": ((16, 64, 1), F32),
        "L1_ffn_bv": ((16, 1, 64), F32),
        "L1_ffn_Wsw_t": ((1024, 8), F32),
        "L1_ffn_bsw_row": ((1, 8), F32),
    }
    ins.update(_cin_specs())
    outs = {"x5": _XS, "hid_b": ((NB, 1024, 2048), BF16), "emb_b": _XS,
            "oh_b": ((NB, 8, 2048), F32), "rp_b": ((NB, 1, 2048), F32),
            "cnt_b": ((NB, 8, 1), F32)}
    scr = {"t4": ((NB, 1024, 2048), F32)}

    def phase(c, cst, dram):
        for b in range(NB):
            emit_l1moe(c, cst, dram, dram["x3"][b], dram["emb"][b],
                       dram["hid"][b], dram["oh"][b], dram["rp"][b],
                       dram["ctbl"], dram["pmask"][b], dram["t4"][b])
            emit_l1attn(c, cst, dram, dram["t4"][b], dram["x5"][b])
            emit_l1router(c, cst, dram, dram["x5"][b], dram["hid_b"][b],
                          dram["emb_b"][b], dram["oh_b"][b], dram["rp_b"][b],
                          dram["cnt_b"][b])
    return build_program([phase], ins, outs,
                         consts_need=("sel8", "allon8", "ident_bf", "ident_f",
                                      "tril_f"),
                         scratch_specs=scr)


def kernel3(**inputs):
    W = _layout_weights(inputs)
    x = np.ascontiguousarray(
        np.asarray(inputs["L0_attn_x"]).transpose(0, 2, 1)).astype(np.float32)
    shards = [np.ascontiguousarray(x[2 * i:2 * i + 2]) for i in range(NCORE)]
    if "l0" not in _PROGS:
        _PROGS["l0"] = _prog_l0()
        _PROGS["mid"] = _prog_mid()
    l0keys = [k for k in W if k.startswith(("L0_", "L1_ffn_W", "L1_ffn_b"))
              and k not in ("L1_ffn_We_t", "L1_ffn_be")]
    midkeys = [k for k in W if k.startswith(("L1_",))]
    pm = np.tril(np.ones((16, 16), np.float32), -1)

    def maps(keys, per):
        out = []
        for i in range(NCORE):
            m = {k: W[k] for k in keys if k in W}
            m.update(_CONSTS)
            m.update(per[i])
            out.append(m)
        return out

    r1 = run_bass_kernel_spmd(_PROGS["l0"],
                              maps(l0keys, [{"xin": shards[i]} for i in range(NCORE)]),
                              core_ids=list(range(NCORE))).results
    ctbl = np.ascontiguousarray(np.concatenate(
        [r1[i]["cnt"][:, :, 0] for i in range(NCORE)], axis=0).astype(np.float32))
    per = []
    for i in range(NCORE):
        per.append({"x3": r1[i]["x3"], "emb": r1[i]["emb"], "hid": r1[i]["hid"],
                    "oh": r1[i]["oh"], "rp": r1[i]["rp"], "ctbl": ctbl,
                    "pmask": np.ascontiguousarray(
                        pm[[2 * i, 2 * i + 1]][:, :, None])})
    r2 = run_bass_kernel_spmd(_PROGS["mid"], maps(midkeys, per),
                              core_ids=list(range(NCORE))).results
    ctbl_b = np.ascontiguousarray(np.concatenate(
        [r2[i]["cnt_b"][:, :, 0] for i in range(NCORE)], axis=0).astype(np.float32))
    per = []
    for i in range(NCORE):
        per.append({"xin": r2[i]["x5"], "emb": r2[i]["emb_b"],
                    "hid": r2[i]["hid_b"], "oh": r2[i]["oh_b"],
                    "rp": r2[i]["rp_b"], "ctbl": ctbl_b,
                    "pmask": np.ascontiguousarray(
                        pm[[2 * i, 2 * i + 1]][:, :, None])})
    mok = ("L1_ffn_We_t", "L1_ffn_be", "L1_ffn_gamma", "L1_ffn_beta")
    r3 = run_bass_kernel_spmd(_prog("moe"), maps(mok, per),
                              core_ids=list(range(NCORE))).results
    full = np.concatenate([r3[i]["xout"] for i in range(NCORE)], axis=0)
    return np.ascontiguousarray(full.transpose(0, 2, 1)).astype(np.float32)


kernel8 = kernel
kernel = kernel3
